# version 09
# speedup vs baseline: 1.0378x; 1.0035x over previous
import numpy as np
import ml_dtypes
import concourse.bass as bass
import concourse.mybir as mybir
from concourse.bass_utils import run_bass_kernel_spmd
from concourse.alu_op_type import AluOpType as ALU

F32 = mybir.dt.float32
BF16 = mybir.dt.bfloat16
AF = mybir.ActivationFunctionType
AX = mybir.AxisListType
BF = ml_dtypes.bfloat16

NCORES = 8
D = 1024
T = 4096
SEQ = 2048
NSEQ = 2
KC = D // 128
NTILE = T // 512
DFF = 4096
MC = DFF // 128
NEXP = 8
NL = 2
GRP = 1024
NGRP = T // GRP
NTG = GRP // 512
IFF = 2816
ICH = IFF // 128
RND_C = float(np.float32(3 * 2 ** 22))

VO = {}
_off = 0
def _vcol(name, n):
    global _off
    VO[name] = _off
    _off += n
_vcol('loglam', KC)
_vcol('b_b', KC)
_vcol('c_b', KC)
for _nm in ('ln1_w', 'ln1_b', 'ln2_w', 'ln2_b', 'bq', 'bk', 'bv', 'bo'):
    _vcol(_nm, NL * KC)
_vcol('e_b1', NL * NEXP * MC)
_vcol('e_b2', NL * NEXP * KC)
NV = _off


class Sem:
    def __init__(self, nc, name):
        self.h = nc.alloc_semaphore(name)
        self.n = 0

    def inc(self, instr, k=1):
        instr.then_inc(self.h, k)
        self.n += k
        return self.n

    def wait(self, eng, n=None):
        eng.wait_ge(self.h, self.n if n is None else n)


_DTSZ = {F32: 4, BF16: 2}


class Ctx:
    def __init__(self, name):
        self.name = name
        self.nc = bass.Bass("TRN2", target_bir_lowering=False)
        self.reads = []
        self.writes = []
        self.dram = {}
        self.sems = {}
        self._psum = None
        self.bufs = {}
        self.consts = None
        self.vecs = None
        self.block = None
        self.arena0 = None
        self.aoff = 0
        self.phase = ""
        self._join_counts = {}

    def arena_start(self):
        self.arena0 = (int(self.nc.sbuf_base) + 31) & ~31

    arena_log = None
    join_exclude = ()

    def begin_phase(self, tag, stack=False):
        if self.arena0 is None:
            self.arena_start()
        if self.arena_log is None:
            self.arena_log = []
        if self.phase:
            self.arena_log.append((self.phase, self.aoff))
            if not stack:
                self.join()
        self.phase = tag
        if not stack:
            self.aoff = 0

    def join(self):
        waits = []
        for nm, s in self.sems.items():
            if nm in self.join_exclude:
                continue
            if s.n > self._join_counts.get(nm, 0):
                waits.append((s.h, s.n))
                self._join_counts[nm] = s.n
        if not waits:
            return
        for eng in ('sync', 'scalar', 'vector', 'tensor', 'gpsimd'):
            @getattr(self.block, eng)
            def _(e, waits=waits):
                for h, n in waits:
                    e.wait_ge(h, n)

    def r(self, name, shape, dtype):
        if name not in self.dram:
            self.dram[name] = self.nc.dram_tensor(name, list(shape), dtype,
                                                  kind="ExternalInput").ap()
            self.reads.append(name)
        return self.dram[name]

    def w(self, name, shape, dtype):
        if name not in self.dram:
            self.dram[name] = self.nc.dram_tensor(name, list(shape), dtype,
                                                  kind="ExternalOutput").ap()
            self.writes.append(name)
        return self.dram[name]

    def internal(self, name, shape, dtype):
        if name not in self.dram:
            self.dram[name] = self.nc.dram_tensor(name, list(shape), dtype).ap()
        return self.dram[name]

    def sem(self, name):
        if name not in self.sems:
            self.sems[name] = Sem(self.nc, name)
        return self.sems[name]

    def sb(self, name, shape, dtype):
        assert self.arena0 is None, f"persistent alloc {name} after arena start"
        return self.nc.alloc_sbuf_tensor(name, list(shape), dtype).ap()

    def buf(self, name, shape, dtype):
        key = self.phase + "." + name
        if key not in self.bufs:
            free = 1
            for d in shape[1:]:
                free *= d
            nbytes = (free * _DTSZ[dtype] + 31) & ~31
            off = self.arena0 + self.aoff
            assert off + nbytes <= 229344, \
                f"SBUF arena overflow: {key} at {off}+{nbytes}"
            self.aoff += nbytes
            self.bufs[key] = self.nc.alloc_sbuf_tensor_at(
                key, list(shape), dtype, offset=off).ap()
        return self.bufs[key]

    def psum(self, i):
        if self._psum is None:
            self._psum = [self.nc.alloc_psum_tensor(f"pb{j}", [128, 512], F32).ap()
                          for j in range(8)]
        return self._psum[i]

    def psum_bf(self, i):
        return self.psum(i).bitcast(BF16)

    @property
    def sv(self):
        return self.sem("g_vec")

    @property
    def sa(self):
        return self.sem("g_act")

    @property
    def sg(self):
        return self.sem("g_gps")


def V(ctx, v, instr):
    return ctx.sv.inc(instr)


def fmview(ap):
    return ap.rearrange("(c p) t -> p c t", p=128)


def emit_consts(ctx):
    block = ctx.block
    c = {}
    s = ctx.sem("s_const")
    c['_sem'] = s
    c['ident'] = ctx.sb("c_ident", [128, 128], BF16)
    c['iota128'] = ctx.sb("c_iota128", [128, 128], F32)
    c['iota512'] = ctx.sb("c_iota512", [128, 512], F32)
    c['masks'] = [ctx.sb(f"c_mask{j}", [128, 512], BF16) for j in range(4)]
    c['ones_b'] = ctx.sb("c_onesb", [128, 1], BF16)
    c['ones_f'] = ctx.sb("c_onesf", [128, 1], F32)
    c['onesr_f'] = ctx.sb("c_onesrf", [1, 128], F32)
    c['eps1'] = ctx.sb("c_eps1", [1, 1], F32)

    @block.gpsimd
    def _(g):
        s.inc(g.iota(c['iota128'][:], pattern=[[-1, 128]], base=0,
                     channel_multiplier=1, allow_small_or_imprecise_dtypes=True))
        s.inc(g.iota(c['iota512'][:], pattern=[[-1, 512]], base=0,
                     channel_multiplier=1, allow_small_or_imprecise_dtypes=True))

    @block.vector
    def _(v):
        v.wait_ge(s.h, 2)
        s.inc(v.tensor_scalar(c['ident'][:], c['iota128'][:], 0.0, None,
                              op0=ALU.is_equal))
        for j in range(4):
            s.inc(v.tensor_scalar(c['masks'][j][:], c['iota512'][:],
                                  float(-128 * j), None, op0=ALU.is_le))
        s.inc(v.memset(c['ones_b'][:], 1.0))
        s.inc(v.memset(c['ones_f'][:], 1.0))
        s.inc(v.memset(c['onesr_f'][:], 1.0))
        s.inc(v.memset(c['eps1'][:], 1e-5))
    ctx.consts = c
    return c


def emit_identF(ctx):
    cst = ctx.consts
    sv = ctx.sv
    identF = ctx.buf("aux_identF", [128, 128], F32)

    @ctx.block.vector
    def _(v):
        cst['_sem'].wait(v)
        sv.wait(v)
        sv.inc(v.tensor_scalar(identF[:], cst['iota128'][:], 0.0, None,
                               op0=ALU.is_equal))
    return identF, sv.n


def emit_onehot_r(ctx, rows, name):
    sv, sg = ctx.sv, ctx.sg
    iota = ctx.buf(f"aux_iota_{name}", [rows, rows * 128], F32)
    oh = ctx.buf(f"aux_oh_{name}", [rows, rows * 128], F32)

    @ctx.block.gpsimd
    def _(g):
        sg.inc(g.iota(iota[:].rearrange("r (e p) -> r e p", e=rows),
                      pattern=[[-1, rows], [0, 128]], base=0,
                      channel_multiplier=1,
                      allow_small_or_imprecise_dtypes=True))
    gm = ctx.sg.n

    @ctx.block.vector
    def _(v):
        v.wait_ge(sg.h, gm)
        sv.wait(v)
        sv.inc(v.tensor_scalar(oh[:], iota[:], 0.0, None, op0=ALU.is_equal))
    return oh, sv.n


def load_vecs(ctx):
    vecs_in = ctx.r("vecs", [128, NV], F32)
    vs = ctx.sb("vecs_sb", [128, NV], F32)
    s = ctx.sem("s_vecs")

    @ctx.block.sync
    def _(sy):
        s.inc(sy.dma_start(out=vs[:], in_=vecs_in[:]), 16)
    ctx.vecs = vs
    return vs


def vcol(vs, name, idx=0):
    o = VO[name] + idx
    return vs[:, o:o + 1]


def emit_ln(ctx, tag, src=None, out_f32=None, out_bf16=None,
            vs=None, wname=None, bname=None, lidx=0,
            gate=None, gatew_sb=None, gateb_col=None, logitsT_sb=None,
            absmax_sb=None, prologue=None, keep_sb=None, keep_f32=None):
    block = ctx.block
    cst = ctx.consts
    sv, sa = ctx.sv, ctx.sa
    affine = wname is not None
    xin = [ctx.buf(f"ln_xin{i}", [128, KC, 512], F32) for i in range(2)]
    tf = [ctx.buf(f"ln_tf{i}", [128, KC, 512], F32) for i in range(2)]
    sq = tf
    outf = [ctx.buf(f"ln_outf{i}", [128, KC, 512], F32) for i in range(2)]
    outb = [ctx.buf(f"ln_outb{i}", [128, KC, 512], BF16) for i in range(2)]
    mrow = [ctx.buf(f"ln_mrow{i}", [1, 512], F32) for i in range(2)]
    vrow = [ctx.buf(f"ln_vrow{i}", [1, 512], F32) for i in range(2)]
    rrow = [ctx.buf(f"ln_rrow{i}", [1, 512], F32) for i in range(2)]
    wrow = [ctx.buf(f"ln_wrow{i}", [1, 512], F32) for i in range(2)]
    if prologue == 'x1':
        yb = outf
        gb = tf
        s0b = [ctx.buf(f"ln_s0b{i}", [128, KC, 512], F32) for i in range(2)]
    if absmax_sb is not None:
        amx = ctx.buf(f"{tag}_amx", [128, KC * NTILE], F32)
    if gate is not None and gate[0] == 'seq':
        hmrows = ctx.buf(f"{tag}_hmr", [128, KC * NTILE], F32)

    sL = ctx.sem("ln_L")
    sMM = ctx.sem("ln_MM")
    sBC = ctx.sem("ln_BC")
    sLG = ctx.sem("ln_LG")
    sOut = ctx.sem("ln_Out")
    marks = {'ap': [], 'out': [], 'ld': [], 'lge': []}
    n_loads = 3 if prologue == 'x1' else 1

    for ti in range(NTILE):
        pb = ti % 2
        tsl = slice(ti * 512, (ti + 1) * 512)

        @block.sync
        def _(sy, ti=ti, pb=pb, tsl=tsl):
            if ti >= 2:
                sy.wait_ge(sv.h, marks['ap'][ti - 2])
                sy.wait_ge(sOut.h, marks['out'][ti - 2])
            if prologue == 'x1':
                sL.inc(sy.dma_start(out=yb[pb][:],
                                    in_=fmview(ctx.dram['yffn1'])[:, :, tsl]), 16)
                sL.inc(sy.dma_start(out=s0b[pb][:],
                                    in_=fmview(ctx.dram['s0'])[:, :, tsl]), 16)
                sL.inc(sy.dma_start(out=xin[pb][:],
                                    in_=fmview(ctx.dram['x_fm'])[:, :, tsl]), 16)
            else:
                sL.inc(sy.dma_start(out=xin[pb][:],
                                    in_=fmview(src)[:, :, tsl]), 16)
        ld_mark = sL.n
        marks['ld'].append(ld_mark)

        if prologue == 'x1':
            @block.scalar
            def _(s, pb=pb, ld=ld_mark):
                s.wait_ge(sL.h, ld)
                sa.wait(s)
                a = None
                for c in range(KC):
                    a = s.activation(gb[pb][:, c], yb[pb][:, c], AF.Silu)
                sa.inc(a)
            silu_mark = sa.n

            @block.vector
            def _(v, pb=pb, sm=silu_mark):
                v.wait_ge(sa.h, sm)
                sv.wait(v)
                ed = None
                for c in range(KC):
                    ed = v.tensor_tensor(gb[pb][:, c], gb[pb][:, c],
                                         s0b[pb][:, c], op=ALU.mult)
                sv.inc(ed)
                sv.wait(v)
                ed = None
                for c in range(KC):
                    ed = v.tensor_tensor(xin[pb][:, c], xin[pb][:, c],
                                         gb[pb][:, c], op=ALU.add)
                sv.inc(ed)
        pro_mark = sv.n

        @block.vector
        def _(v, pb=pb, ld=ld_mark):
            if prologue != 'x1':
                v.wait_ge(sL.h, ld)
            sv.wait(v)
            ed = None
            for c in range(KC):
                ed = v.tensor_tensor(sq[pb][:, c], xin[pb][:, c], xin[pb][:, c],
                                     op=ALU.mult)
            sv.inc(ed)
        sq_mark = sv.n

        @block.tensor
        def _(t, ti=ti, pb=pb, ld=ld_mark, pm=pro_mark, sqm=sq_mark):
            cst['_sem'].wait(t)
            if ti >= 2:
                t.wait_ge(sv.h, marks['ap'][ti - 2])
            if prologue == 'x1':
                t.wait_ge(sv.h, pm)
            else:
                t.wait_ge(sL.h, ld)
            p_s1 = ctx.psum(pb * 2)
            p_s2 = ctx.psum(pb * 2 + 1)
            mm = None
            for c in range(KC):
                mm = t.matmul(p_s1[0:1, :], cst['ones_f'][:], xin[pb][:, c],
                              start=(c == 0), stop=(c == KC - 1))
            t.wait_ge(sv.h, sqm)
            for c in range(KC):
                mm = t.matmul(p_s2[0:1, :], cst['ones_f'][:], sq[pb][:, c],
                              start=(c == 0), stop=(c == KC - 1))
            sMM.inc(mm)
        mm_mark = sMM.n

        @block.vector
        def _(v, pb=pb, mmm=mm_mark):
            v.wait_ge(sMM.h, mmm)
            sv.wait(v)
            p_s1 = ctx.psum(pb * 2)[0:1, :]
            p_s2 = ctx.psum(pb * 2 + 1)[0:1, :]
            sv.inc(v.tensor_scalar(mrow[pb][:], p_s1, 1.0 / D, None, op0=ALU.mult))
            sv.wait(v)
            sv.inc(v.tensor_tensor(wrow[pb][:], mrow[pb][:], mrow[pb][:],
                                   op=ALU.mult))
            sv.wait(v)
            sv.inc(v.scalar_tensor_tensor(vrow[pb][:], p_s2, 1.0 / D, wrow[pb][:],
                                          op0=ALU.mult, op1=ALU.subtract))
        var_mark = sv.n

        @block.scalar
        def _(s, pb=pb, vm=var_mark):
            s.wait_ge(sv.h, vm)
            sa.wait(s)
            sa.inc(s.activation(vrow[pb][:], vrow[pb][:], AF.Ln,
                                bias=cst['eps1'][:]))
            sa.wait(s)
            sa.inc(s.activation(rrow[pb][:], vrow[pb][:], AF.Exp, scale=-0.5))
        act_mark = sa.n

        @block.vector
        def _(v, pb=pb, am=act_mark):
            v.wait_ge(sa.h, am)
            sv.wait(v)
            sv.inc(v.tensor_tensor(wrow[pb][:], mrow[pb][:], rrow[pb][:],
                                   op=ALU.mult))
        st_mark = sv.n

        @block.tensor
        def _(t, ti=ti, pb=pb, stm=st_mark):
            t.wait_ge(sv.h, stm)
            if ti >= 1:
                t.wait_ge(sv.h, marks['ap'][ti - 1])
            t.matmul(ctx.psum(4)[:, :], cst['onesr_f'][:], rrow[pb][:],
                     start=True, stop=True)
            mm = t.matmul(ctx.psum(5)[:, :], cst['onesr_f'][:], wrow[pb][:],
                          start=True, stop=True)
            sBC.inc(mm)
        bc_mark = sBC.n

        @block.vector
        def _(v, ti=ti, pb=pb, bcm=bc_mark, tsl=tsl):
            v.wait_ge(sBC.h, bcm)
            if ti >= 2:
                v.wait_ge(sOut.h, marks['out'][ti - 2])
            sv.wait(v)
            pbc1 = ctx.psum(4)
            pbc2 = ctx.psum(5)
            ed = None
            for c in range(KC):
                ed = v.tensor_tensor(tf[pb][:, c], xin[pb][:, c], pbc1[:, :],
                                     op=ALU.mult)
            sv.inc(ed)
            sv.wait(v)
            dst = tf if affine else outf
            ed = None
            for c in range(KC):
                ed = v.tensor_tensor(dst[pb][:, c], tf[pb][:, c], pbc2[:, :],
                                     op=ALU.subtract)
            sv.inc(ed)
            sv.wait(v)
            if affine:
                ed = None
                for c in range(KC):
                    ed = v.tensor_scalar(outf[pb][:, c], tf[pb][:, c],
                                         vcol(vs, wname, lidx * KC + c),
                                         vcol(vs, bname, lidx * KC + c),
                                         op0=ALU.mult, op1=ALU.add)
                sv.inc(ed)
                sv.wait(v)
            if out_bf16 is not None or keep_sb is not None:
                ed = None
                for c in range(KC):
                    dstt = (keep_sb[:, c, tsl] if keep_sb is not None
                            else outb[pb][:, c])
                    ed = v.tensor_copy(dstt, outf[pb][:, c])
                sv.inc(ed)
                sv.wait(v)
            if keep_f32 is not None:
                ed = None
                for c in range(KC):
                    ed = v.tensor_copy(keep_f32[:, c, tsl], outf[pb][:, c])
                sv.inc(ed)
                sv.wait(v)
            if absmax_sb is not None:
                ed = None
                for c in range(KC):
                    ed = v.tensor_reduce(amx[:, ti * KC + c:ti * KC + c + 1],
                                         outf[pb][:, c], axis=AX.X, op=ALU.max,
                                         apply_absolute_value=True)
                sv.inc(ed)
                sv.wait(v)
            if gate is not None and gate[0] == 'seq':
                ed = None
                for c in range(KC):
                    ed = v.tensor_reduce(hmrows[:, ti * KC + c:ti * KC + c + 1],
                                         outf[pb][:, c], axis=AX.X, op=ALU.add)
                sv.inc(ed)
                sv.wait(v)
        marks['ap'].append(sv.n)

        if gate is not None and gate[0] == 'token':
            @block.tensor
            def _(t, ti=ti, pb=pb):
                t.wait_ge(sv.h, marks['ap'][ti])
                if ti >= 2:
                    t.wait_ge(sa.h, marks['lge'][ti - 2])
                plg = ctx.psum(6 + pb)
                mm = None
                for c in range(KC):
                    mm = t.matmul(plg[0:8, :], gatew_sb[:, c * 8:(c + 1) * 8],
                                  outf[pb][:, c], start=(c == 0),
                                  stop=(c == KC - 1))
                sLG.inc(mm)
            lg_mark = sLG.n

            @block.scalar
            def _(s, pb=pb, lgm=lg_mark, tsl=tsl):
                s.wait_ge(sLG.h, lgm)
                sa.wait(s)
                plg = ctx.psum(6 + pb)
                sa.inc(s.activation(logitsT_sb[0:8, tsl], plg[0:8, :], AF.Copy,
                                    bias=0.0))
            marks['lge'].append(sa.n)

        @block.sync
        def _(sy, ti=ti, pb=pb, tsl=tsl, pm=pro_mark):
            sy.wait_ge(sv.h, marks['ap'][ti])
            if prologue == 'x1':
                sOut.inc(sy.dma_start(out=fmview(ctx.dram['x1'])[:, :, tsl],
                                      in_=xin[pb][:]), 16)
            if out_f32 is not None:
                sOut.inc(sy.dma_start(out=fmview(out_f32)[:, :, tsl],
                                      in_=outf[pb][:]), 16)
            if out_bf16 is not None:
                srcb = (keep_sb[:, :, tsl] if keep_sb is not None else outb[pb][:])
                sOut.inc(sy.dma_start(out=fmview(out_bf16)[:, :, tsl],
                                      in_=srcb), 16)
        marks['out'].append(sOut.n)

    if gate is not None and gate[0] == 'token':
        @block.vector
        def _(v):
            v.wait_ge(sa.h, marks['lge'][-1])
            sv.wait(v)
            sv.inc(v.tensor_scalar(logitsT_sb[0:8, :], logitsT_sb[0:8, :],
                                   gateb_col, None, op0=ALU.add))
        marks['lgdone'] = sv.n
    if absmax_sb is not None:
        @block.vector
        def _(v):
            sv.wait(v)
            sv.inc(v.tensor_reduce(absmax_sb, amx[:], axis=AX.X, op=ALU.max))
        marks['amdone'] = sv.n
    if gate is not None and gate[0] == 'seq':
        hmean = ctx.buf(f"{tag}_hmean", [128, KC * NSEQ], F32)

        @block.vector
        def _(v):
            sv.wait(v)
            hmv = hmrows[:].rearrange("p (t c) -> p t c", t=NTILE)
            ed = None
            for s_ in range(NSEQ):
                for c in range(KC):
                    ed = v.tensor_reduce(
                        hmean[:, c * NSEQ + s_:c * NSEQ + s_ + 1],
                        hmv[:, s_ * 4:(s_ + 1) * 4, c], axis=AX.X, op=ALU.add)
            sv.inc(ed)
            sv.wait(v)
            sv.inc(v.tensor_scalar(hmean[:], hmean[:], 1.0 / SEQ, None,
                                   op0=ALU.mult))
        marks['hmdone'] = sv.n
        marks['hmean'] = hmean
    return marks


def emit_p0(ctx):
    nc = ctx.nc
    bwT = ctx.r("bwT", [D, D], F32)
    cwT = ctx.r("cwT", [D, D], F32)
    attw = ctx.r("attw", [NL, 4, D, D], F32)
    e_w1 = ctx.r("e_w1", [NL, NEXP, D, DFF], F32)
    e_w2 = ctx.r("e_w2", [NL, NEXP, DFF, D], F32)
    W1T = ctx.r("W1T", [D, 3 * D], F32)
    W2T = ctx.r("W2T", [D, D], F32)
    wg = ctx.r("wg", [IFF, D], F32)
    wu = ctx.r("wu", [IFF, D], F32)
    wd = ctx.r("wd", [D, IFF], F32)
    bwT_bf = ctx.internal("bwT_bf", [D, D], BF16)
    cwT_bf = ctx.internal("cwT_bf", [D, D], BF16)
    attw_bf = ctx.internal("attw_bf", [NL, 4, D, D], BF16)
    e_w1bf = ctx.internal("e_w1bf", [NL, NEXP, D, DFF], BF16)
    e_w2bf = ctx.internal("e_w2bf", [NL, NEXP, DFF, D], BF16)
    W1T_bf = ctx.internal("W1T_bf", [D, 3 * D], BF16)
    W2T_bf = ctx.internal("W2T_bf", [D, D], BF16)
    wgq = ctx.internal("wgq", [IFF, D], BF16)
    wuq = ctx.internal("wuq", [IFF, D], BF16)
    wdq = ctx.internal("wdq", [D, IFF], BF16)
    qscale = ctx.internal("qscale", [128, 2 * ICH + KC], F32)

    if True:
        block = ctx.block
        sCast = ctx.sem("p0_Cast")
        cm = {}
        ctx.cast_marks = cm

        @block.gpsimd
        def _(g):
            sCast.inc(g.dma_start(out=bwT_bf[:], in_=bwT[:]), 16)
            sCast.inc(g.dma_start(out=cwT_bf[:], in_=cwT[:]), 16)
            cm['bwcw'] = sCast.n
            for l in range(NL):
                for j in range(4):
                    sCast.inc(g.dma_start(out=attw_bf[l, j], in_=attw[l, j]), 16)
                cm[f'attw{l}'] = sCast.n
            sCast.inc(g.dma_start(out=W1T_bf[:], in_=W1T[:]), 16)
            sCast.inc(g.dma_start(out=W2T_bf[:], in_=W2T[:]), 16)
            cm['w12'] = sCast.n
            for l in range(NL):
                if l == 0:
                    g.wait_ge(ctx.sem("at_Z").h, 16)
                else:
                    g.wait_ge(ctx.sem("moe_H").h, 16)
                for e in range(NEXP):
                    sCast.inc(g.dma_start(out=e_w1bf[l, e], in_=e_w1[l, e]), 16)
                    sCast.inc(g.dma_start(out=e_w2bf[l, e], in_=e_w2[l, e]), 16)
                cm[f'ew{l}'] = sCast.n
    return ctx


def emit_p0_quant(ctx):
    wg = ctx.r("wg", [IFF, D], F32)
    wu = ctx.r("wu", [IFF, D], F32)
    wd = ctx.r("wd", [D, IFF], F32)
    wgq = ctx.internal("wgq", [IFF, D], BF16)
    wuq = ctx.internal("wuq", [IFF, D], BF16)
    wdq = ctx.internal("wdq", [D, IFF], BF16)
    qscale = ctx.internal("qscale", [128, 2 * ICH + KC], F32)

    if True:
        block = ctx.block
        sv = ctx.sv
        wtile = [ctx.buf(f"p0_w{i}", [128, IFF], F32) for i in range(2)]
        qtile = [ctx.buf(f"p0_q{i}", [128, IFF], BF16) for i in range(2)]
        rmax = ctx.buf("p0_rmax", [128, 2], F32)
        rinv = ctx.buf("p0_rinv", [128, 2], F32)
        qsc = ctx.buf("p0_qsc", [128, 2 * ICH + KC], F32)
        sLd = ctx.sem("p0_Ld")
        sSt = ctx.sem("p0_St")
        qmarks = []
        stmarks = []

        jobs = [(wg, D, IFF // 128, wgq, 0), (wu, D, IFF // 128, wuq, ICH),
                (wd, IFF, D // 128, wdq, 2 * ICH)]
        gi = 0
        for src, C, nr, qdst, sc0 in jobs:
            for rc in range(nr):
                pb = gi % 2

                @block.sync
                def _(sy, src=src, C=C, rc=rc, pb=pb, gi=gi):
                    if gi >= 2:
                        sy.wait_ge(sSt.h, stmarks[gi - 2])
                    sLd.inc(sy.dma_start(out=wtile[pb][:, :C],
                                         in_=src[rc * 128:(rc + 1) * 128, :]), 16)
                ld_mark = sLd.n

                @block.vector
                def _(v, C=C, rc=rc, pb=pb, sc0=sc0, ld=ld_mark):
                    v.wait_ge(sLd.h, ld)
                    sv.wait(v)
                    wt = wtile[pb][:, :C]
                    qt = qtile[pb][:, :C]
                    sv.inc(v.tensor_reduce(rmax[:, pb:pb + 1], wt, axis=AX.X,
                                           op=ALU.max, apply_absolute_value=True))
                    sv.wait(v)
                    sv.inc(v.tensor_scalar(qsc[:, sc0 + rc:sc0 + rc + 1],
                                           rmax[:, pb:pb + 1], 1.0 / 127.0, None,
                                           op0=ALU.mult))
                    sv.wait(v)
                    sv.inc(v.reciprocal(rinv[:, pb:pb + 1],
                                        qsc[:, sc0 + rc:sc0 + rc + 1]))
                    sv.wait(v)
                    sv.inc(v.tensor_scalar(qt, wt, rinv[:, pb:pb + 1], None,
                                           op0=ALU.mult))
                    sv.wait(v)
                    sv.inc(v.tensor_scalar(qt, qt, RND_C, RND_C,
                                           op0=ALU.add, op1=ALU.subtract))
                    sv.wait(v)
                    sv.inc(v.tensor_scalar(qt, qt, -128.0, 127.0,
                                           op0=ALU.max, op1=ALU.min))
                qmarks.append(sv.n)

                @block.sync
                def _(sy, C=C, rc=rc, pb=pb, qdst=qdst, gi=gi):
                    sy.wait_ge(sv.h, qmarks[gi])
                    sSt.inc(sy.dma_start(out=qdst[rc * 128:(rc + 1) * 128, :],
                                         in_=qtile[pb][:, :C]), 16)
                stmarks.append(sSt.n)
                gi += 1

        @block.sync
        def _(sy):
            sy.wait_ge(sv.h, qmarks[-1])
            sSt.inc(sy.dma_start(out=qscale[:], in_=qsc[:]), 16)
    return ctx


def emit_p12(ctx):
    nc = ctx.nc
    x_fm = ctx.r("x_fm", [D, T], F32)
    bwT_bf = ctx.internal("bwT_bf", [D, D], BF16)
    cwT_bf = ctx.internal("cwT_bf", [D, D], BF16)
    xn0f = ctx.internal("xn0f", [D, T], F32)
    xn0b = ctx.internal("xn0b", [D, T], BF16)
    s0 = ctx.internal("s0", [D, T], F32)

    if True:
        block = ctx.block
        cst = ctx.consts
        vs = ctx.vecs
        sv, sa = ctx.sv, ctx.sa
        lnm = emit_ln(ctx, "ln0", src=x_fm, out_f32=xn0f, out_bf16=xn0b)
        ln_done = lnm['ap'][-1]
        ctx.begin_phase('p12b')

        bw_sb = ctx.buf("bw_sb", [128, KC, D], BF16)
        cw_sb = ctx.buf("cw_sb", [128, KC, D], BF16)
        lam = ctx.buf("lam_sb", [128, KC], F32)
        oml = ctx.buf("oml_sb", [128, KC], F32)
        d1 = ctx.buf("d1_sb", [128, T], F32)
        h_sb = ctx.buf("h_sb", [128, KC, T], BF16)
        xnt = [ctx.buf(f"xnt{i}", [128, KC, 512], BF16) for i in range(2)]
        s0t = [d1[:, i * 512:(i + 1) * 512] for i in range(2)]
        sXT = ctx.sem("p2_XT")
        sWL = ctx.sem("p2_WL")
        sPM = ctx.sem("p2_PM")
        sS0 = ctx.sem("p2_S0")

        @block.sync
        def _(sy):
            sy.wait_ge(ctx.sem("p0_Cast").h, ctx.cast_marks['bwcw'])
            sWL.inc(sy.dma_start(out=bw_sb[:],
                                 in_=bwT_bf.rearrange("(k p) e -> p k e", p=128)), 16)
            sWL.inc(sy.dma_start(out=cw_sb[:],
                                 in_=cwT_bf.rearrange("(k p) e -> p k e", p=128)), 16)

        @block.scalar
        def _(s):
            ctx.sem("s_vecs").wait(s)
            sa.wait(s)
            lo = VO['loglam']
            sa.inc(s.activation(lam[:], vs[:, lo:lo + KC], AF.Exp))
            sa.wait(s)
            sa.inc(s.activation(lam[:], lam[:], AF.Sigmoid))
        lam_mark = sa.n

        @block.vector
        def _(v):
            v.wait_ge(sa.h, lam_mark)
            sv.wait(v)
            sv.inc(v.tensor_scalar(oml[:], lam[:], -1.0, 1.0,
                                   op0=ALU.mult, op1=ALU.add))
        oml_mark = sv.n

        d1marks = {}
        ldmarks = {}
        for ec in range(KC):
            for ti in range(NTILE):
                pb = ti % 2

                @block.sync
                def _(sy, ec=ec, ti=ti, pb=pb):
                    gi = ec * NTILE + ti
                    if ec == 0:
                        sy.wait_ge(ctx.sem("ln_Out").h, lnm['out'][ti])
                    if gi >= 2:
                        prev = (ec, ti - 2) if ti >= 2 else (ec - 1, NTILE - 2 + ti)
                        sy.wait_ge(sPM.h, d1marks.get(('mm',) + prev, 0))
                    sXT.inc(sy.dma_start(
                        out=xnt[pb][:],
                        in_=fmview(xn0b)[:, :, ti * 512:(ti + 1) * 512]), 16)
                ldmarks[(ec, ti)] = sXT.n

                @block.tensor
                def _(t, ec=ec, ti=ti, pb=pb):
                    t.wait_ge(sXT.h, ldmarks[(ec, ti)])
                    if ec == 0 and ti == 0:
                        t.wait_ge(sWL.h, 32)
                    if (ec, ti) not in ((0, 0), (0, 1)):
                        prev = (ec, ti - 2) if ti >= 2 else (ec - 1, NTILE - 2 + ti)
                        t.wait_ge(sv.h, d1marks[prev])
                    ps = ctx.psum(pb)
                    mm = None
                    for k in range(KC):
                        mm = t.matmul(ps[:, :], bw_sb[:, k, ec * 128:(ec + 1) * 128],
                                      xnt[pb][:, k], start=(k == 0),
                                      stop=(k == KC - 1))
                    sPM.inc(mm)
                pm_mark = sPM.n
                d1marks[('mm', ec, ti)] = pm_mark

                @block.vector
                def _(v, ec=ec, ti=ti, pb=pb, pmm=pm_mark):
                    v.wait_ge(sPM.h, pmm)
                    if ec == 0 and ti == 0:
                        v.wait_ge(sv.h, oml_mark)
                    sv.wait(v)
                    sv.inc(v.tensor_scalar(d1[:, ti * 512:(ti + 1) * 512],
                                           ctx.psum(pb)[:, :],
                                           vcol(vs, 'b_b', ec),
                                           oml[:, ec:ec + 1],
                                           op0=ALU.add, op1=ALU.mult))
                d1marks[(ec, ti)] = sv.n

            for s_ in range(NSEQ):
                @block.vector
                def _(v, ec=ec, s_=s_):
                    sv.wait(v)
                    ed = v.tensor_tensor_scan(
                        h_sb[:, ec, s_ * SEQ:(s_ + 1) * SEQ],
                        lam[:, ec:ec + 1].broadcast_to((128, SEQ)),
                        d1[:, s_ * SEQ:(s_ + 1) * SEQ], 0.0,
                        op0=ALU.mult, op1=ALU.add)
                    sv.inc(ed)

        scan_done = sv.n

        cev = {}
        for dc in range(KC):
            for ti in range(NTILE):
                pb = ti % 2
                gi = dc * NTILE + ti

                @block.tensor
                def _(t, dc=dc, ti=ti, pb=pb, gi=gi):
                    if gi == 0:
                        t.wait_ge(sv.h, scan_done)
                    if gi >= 2:
                        t.wait_ge(sv.h, cev[gi - 2])
                        t.wait_ge(sS0.h, 16 * (gi - 1))
                    ps = ctx.psum(2 + pb)
                    mm = None
                    for k in range(KC):
                        mm = t.matmul(ps[:, :], cw_sb[:, k, dc * 128:(dc + 1) * 128],
                                      h_sb[:, k, ti * 512:(ti + 1) * 512],
                                      start=(k == 0), stop=(k == KC - 1))
                    sPM.inc(mm)
                pm2 = sPM.n

                @block.vector
                def _(v, dc=dc, pb=pb, pmm=pm2):
                    v.wait_ge(sPM.h, pmm)
                    sv.wait(v)
                    sv.inc(v.tensor_scalar(s0t[pb], ctx.psum(2 + pb)[:, :],
                                           vcol(vs, 'c_b', dc), None, op0=ALU.add))
                cev[gi] = sv.n
                ev_mark = sv.n

                @block.sync
                def _(sy, dc=dc, ti=ti, pb=pb, evm=ev_mark):
                    sy.wait_ge(sv.h, evm)
                    sS0.inc(sy.dma_start(
                        out=fmview(s0)[:, dc, ti * 512:(ti + 1) * 512],
                        in_=s0t[pb]), 16)

        @block.sync
        def _(sy):
            sy.wait_ge(sS0.h, sS0.n)
            sy.wait_ge(ctx.sem("ln_Out").h, ctx.sem("ln_Out").n)
    return ctx


def emit_attn(ctx, l, yin, yout):
    nc = ctx.nc
    zb = ctx.internal("zb", [D, T], BF16)
    y_in = ctx.internal(yin, [D, T], F32)
    attw_bf = ctx.internal("attw_bf", [NL, 4, D, D], BF16)
    attw = ctx.r("attw", [NL, 4, D, D], F32)
    y_out = ctx.internal(yout, [D, T], F32)

    if True:
        block = ctx.block
        cst = ctx.consts
        vs = ctx.vecs
        sv, sa = ctx.sv, ctx.sa
        zseq = ctx.buf("at_z", [128, KC, SEQ], BF16)
        ysb = zseq
        qsb = ctx.buf("at_q", [128, KC, SEQ], BF16)
        ksb = ctx.buf("at_k", [128, KC, SEQ], BF16)
        vtm = ctx.buf("at_v", [128, 16, 1024], BF16)
        wbuf = ctx.buf("at_w", [128, KC, D], BF16)
        ea = ctx.buf("at_ea", [128, 16, 512], BF16)
        eb = ctx.buf("at_eb", [128, 16, 512], BF16)
        rb_sb = ctx.buf("at_rb", [128, 512], F32)
        rrows = ctx.buf("at_rr", [1, 1024], F32)
        ra_row = rrows[:, 0:512]
        rb_row = rrows[:, 512:1024]
        qbias = ctx.buf("at_qb", [128, KC], F32)
        bfin = ctx.buf("at_bf", [128, KC], F32)
        yres = [ctx.buf(f"at_yr{i}", [128, 512], F32) for i in range(2)]
        ytmp = [ctx.buf(f"at_yt{i}", [128, 512], F32) for i in range(2)]
        wfp = ctx.buf("at_wfp", [128, 128], F32)

        sZ = ctx.sem("at_Z")
        sW = ctx.sem("at_W")
        sQKV = ctx.sem("at_QKV")
        sSC = ctx.sem("at_SC")
        sDN = ctx.sem("at_DN")
        sAV = ctx.sem("at_AV")
        sBCp = ctx.sem("at_BC")
        sOP = ctx.sem("at_OP")
        sYL = ctx.sem("at_YL")
        sYS = ctx.sem("at_YS")
        sWF = ctx.sem("at_WF")
        sBF = ctx.sem("at_BFM")
        qkv0 = sQKV.n
        sc0 = sSC.n
        ys0 = sYS.n
        evmarks = []
        expmarks = []
        avmarks = []
        opmarks = []

        @block.vector
        def _(v):
            ctx.sem("s_vecs").wait(v)
            sv.wait(v)
            sv.inc(v.tensor_scalar(qbias[:], vs[:, VO['bq'] + l * KC:
                                                VO['bq'] + (l + 1) * KC],
                                   0.125, None, op0=ALU.mult))
        qb_mark = sv.n
        for dc in range(KC):
            for k in range(KC):
                @block.sync
                def _(sy, dc=dc, k=k):
                    if not (dc == 0 and k == 0):
                        sy.wait_ge(sBF.h, sBF.n)
                    sWF.inc(sy.dma_start(
                        out=wfp[:],
                        in_=attw[l, 3][k * 128:(k + 1) * 128,
                                       dc * 128:(dc + 1) * 128]), 16)

                @block.tensor
                def _(t, dc=dc, k=k):
                    t.wait_ge(sWF.h, sWF.n)
                    ctx.sem("s_vecs").wait(t)
                    mm = t.matmul(ctx.psum(7)[:, 0:1], wfp[:],
                                  vcol(vs, 'bv', l * KC + k),
                                  start=(k == 0), stop=(k == KC - 1))
                    if k == KC - 1:
                        pass
                    sBF.inc(mm)

            @block.vector
            def _(v, dc=dc):
                v.wait_ge(sBF.h, sBF.n)
                sv.wait(v)
                sv.inc(v.tensor_scalar(bfin[:, dc:dc + 1], ctx.psum(7)[:, 0:1],
                                       vcol(vs, 'bo', l * KC + dc), None,
                                       op0=ALU.add))
        bfin_mark = sv.n

        for s_ in range(NSEQ):
            ssl = slice(s_ * SEQ, (s_ + 1) * SEQ)

            @block.sync
            def _(sy, s_=s_, ssl=ssl):
                if s_ > 0:
                    sy.wait_ge(sOP.h, sOP.n)
                sZ.inc(sy.dma_start(out=zseq[:], in_=fmview(zb)[:, :, ssl]), 16)
            z_mark = sZ.n

            for wi, (dst, bias_col0, scl) in enumerate(
                    [(qsb, VO['bq'] + l * KC, 0.125),
                     (ksb, VO['bk'] + l * KC, 1.0),
                     (vtm, None, 1.0)]):
                @block.sync
                def _(sy, wi=wi):
                    if not (s_ == 0 and wi == 0):
                        sy.wait_ge(sQKV.h, sQKV.n)
                    sy.wait_ge(ctx.sem("p0_Cast").h, ctx.cast_marks[f'attw{l}'])
                    sW.inc(sy.dma_start(
                        out=wbuf[:],
                        in_=attw_bf[l, wi].rearrange("(k p) o -> p k o", p=128)), 16)
                w_mark = sW.n

                if wi < 2:
                    for oc in range(KC):
                        for ti in range(4):
                            pb = (oc * 4 + ti) % 4

                            @block.tensor
                            def _(t, wi=wi, oc=oc, ti=ti, pb=pb, wm=w_mark,
                                  zm=z_mark):
                                t.wait_ge(sW.h, wm)
                                t.wait_ge(sZ.h, zm)
                                gi = sQKV.n - qkv0
                                if gi >= 4:
                                    t.wait_ge(sa.h, evmarks[gi - 4])
                                ps = ctx.psum(pb)
                                mm = None
                                for k in range(KC):
                                    mm = t.matmul(
                                        ps[:, :],
                                        wbuf[:, k, oc * 128:(oc + 1) * 128],
                                        zseq[:, k, ti * 512:(ti + 1) * 512],
                                        start=(k == 0), stop=(k == KC - 1))
                                sQKV.inc(mm)
                            q_mark = sQKV.n

                            @block.scalar
                            def _(s, wi=wi, oc=oc, ti=ti, pb=pb, qm=q_mark):
                                s.wait_ge(sQKV.h, qm)
                                if wi == 0:
                                    s.wait_ge(sv.h, qb_mark)
                                sa.wait(s)
                                dst_ = qsb if wi == 0 else ksb
                                bias = (qbias[:, oc:oc + 1] if wi == 0 else
                                        vcol(vs, 'bk', l * KC + oc))
                                scale = 0.125 if wi == 0 else 1.0
                                sa.inc(s.activation(
                                    dst_[:, oc, ti * 512:(ti + 1) * 512],
                                    ctx.psum(pb)[:, :], AF.Identity,
                                    bias=bias, scale=scale))
                            evmarks.append(sa.n)
                else:
                    for tr in range(16):
                        for vd in range(2):
                            pb = (tr * 2 + vd) % 4

                            @block.tensor
                            def _(t, tr=tr, vd=vd, pb=pb, wm=w_mark, zm=z_mark):
                                t.wait_ge(sW.h, wm)
                                t.wait_ge(sZ.h, zm)
                                gi = sQKV.n - qkv0
                                if gi >= 4:
                                    t.wait_ge(sa.h, evmarks[gi - 4])
                                ps = ctx.psum(pb)
                                mm = None
                                for k in range(KC):
                                    mm = t.matmul(
                                        ps[:, :],
                                        zseq[:, k, tr * 128:(tr + 1) * 128],
                                        wbuf[:, k, vd * 512:(vd + 1) * 512],
                                        start=(k == 0), stop=(k == KC - 1))
                                sQKV.inc(mm)
                            v_mark = sQKV.n

                            @block.scalar
                            def _(s, tr=tr, vd=vd, pb=pb, vm=v_mark):
                                s.wait_ge(sQKV.h, vm)
                                sa.wait(s)
                                sa.inc(s.activation(
                                    vtm[:, tr, vd * 512:(vd + 1) * 512],
                                    ctx.psum(pb)[:, :], AF.Identity))
                            evmarks.append(sa.n)
            qkv_done = sa.n

            core_base = sa.n
            pairs = 0
            NIT = KC * 4
            exp_done = [0] * NIT
            av_base = sAV.n

            def sc_part(it):
                nonlocal pairs
                hp, tqt = divmod(it, 4)
                tq = slice(tqt * 512, (tqt + 1) * 512)
                ntk = 4 * (tqt + 1)
                pairs0 = pairs
                scm_l = []

                @block.tensor
                def _(t, hp=hp, tqt=tqt, ntk=ntk, tq=tq, pairs0=pairs0):
                    if hp == 0 and tqt == 0:
                        t.wait_ge(sa.h, qkv_done)
                        t.wait_ge(sv.h, bfin_mark)
                    for tk in range(ntk):
                        p = pairs0 + tk
                        if p >= 2:
                            t.wait_ge(sa.h, core_base + p - 1)
                        mm = t.matmul(ctx.psum(4 + (tk % 2))[:, :],
                                      ksb[0:64, hp, tk * 128:(tk + 1) * 128],
                                      qsb[0:64, hp, tq],
                                      start=True, stop=True)
                        mm = t.matmul(ctx.psum(6 + (tk % 2))[:, :],
                                      ksb[64:128, hp, tk * 128:(tk + 1) * 128],
                                      qsb[64:128, hp, tq],
                                      start=True, stop=True)
                        sSC.inc(mm)
                        scm_l.append(sSC.n)

                @block.scalar
                def _(s, it=it, ntk=ntk):
                    if it >= 1:
                        s.wait_ge(sAV.h, av_base + it)
                    for tk in range(ntk):
                        s.wait_ge(sSC.h, scm_l[tk])
                        s.activation(ea[:, tk, :],
                                     ctx.psum(4 + (tk % 2))[:, :], AF.Exp)
                        sa.inc(s.activation(eb[:, tk, :],
                                            ctx.psum(6 + (tk % 2))[:, :],
                                            AF.Exp))
                        expmarks.append(sa.n)
                pairs += ntk
                exp_done[it] = sa.n
                assert sa.n == core_base + pairs

            def tail_a(it):
                hp, tqt = divmod(it, 4)
                ntk = 4 * (tqt + 1)

                @block.vector
                def _(v, it=it, tqt=tqt):
                    v.wait_ge(sa.h, exp_done[it])
                    sv.wait(v)
                    ed = None
                    for j in range(4):
                        tk = 4 * tqt + j
                        ed = v.tensor_tensor(ea[:, tk, :], ea[:, tk, :],
                                             cst['masks'][j][:], op=ALU.mult)
                    for j in range(4):
                        tk = 4 * tqt + j
                        ed = v.tensor_tensor(eb[:, tk, :], eb[:, tk, :],
                                             cst['masks'][j][:], op=ALU.mult)
                    sv.inc(ed)
                mask_mark = sv.n

                avb = 0 if it % 2 == 0 else 2
                dnb = 1 if it % 2 == 0 else 3

                @block.tensor
                def _(t, it=it, hp=hp, ntk=ntk, mm_=mask_mark, avb=avb,
                      dnb=dnb):
                    t.wait_ge(sv.h, mm_)
                    cst['_sem'].wait(t)
                    if it <= 1:
                        t.wait_ge(sa.h, qkv_done)
                    if it >= 2:
                        t.wait_ge(sv.h, avmarks[it - 2])
                    mm = None
                    for tk in range(ntk):
                        mm = t.matmul(ctx.psum(dnb)[0:1, :], cst['ones_b'][:],
                                      ea[:, tk, :], start=(tk == 0),
                                      stop=(tk == ntk - 1))
                    sDN.inc(mm)
                    for tk in range(ntk):
                        mm = t.matmul(ctx.psum(dnb)[32:33, :], cst['ones_b'][:],
                                      eb[:, tk, :], start=(tk == 0),
                                      stop=(tk == ntk - 1))
                    sDN.inc(mm)
                    for tk in range(ntk):
                        mm = t.matmul(ctx.psum(avb)[0:64, :],
                                      vtm[:, tk, hp * 128:hp * 128 + 64],
                                      ea[:, tk, :], start=(tk == 0),
                                      stop=(tk == ntk - 1))
                    for tk in range(ntk):
                        mm = t.matmul(ctx.psum(avb)[64:128, :],
                                      vtm[:, tk, hp * 128 + 64:(hp + 1) * 128],
                                      eb[:, tk, :], start=(tk == 0),
                                      stop=(tk == ntk - 1))
                    sAV.inc(mm)
                dn_mark = sDN.n
                av_mark = sAV.n
                assert sAV.n == av_base + it + 1

                @block.vector
                def _(v, dnm=dn_mark, dnb=dnb):
                    v.wait_ge(sDN.h, dnm)
                    sv.wait(v)
                    sv.inc(v.reciprocal(ra_row, ctx.psum(dnb)[0:1, :]))
                    sv.wait(v)
                    sv.inc(v.reciprocal(rb_row, ctx.psum(dnb)[32:33, :]))
                tail_state[it] = (sv.n, av_mark)

            def tail_b(it):
                hp, tqt = divmod(it, 4)
                tq = slice(tqt * 512, (tqt + 1) * 512)
                avb = 0 if it % 2 == 0 else 2
                dnb = 1 if it % 2 == 0 else 3
                rc_mark, av_mark = tail_state[it]

                @block.tensor
                def _(t, rcm=rc_mark, dnb=dnb):
                    t.wait_ge(sv.h, rcm)
                    t.matmul(ctx.psum(dnb)[0:64, :], cst['onesr_f'][:, 0:64],
                             ra_row, start=True, stop=True)
                    mm = t.matmul(ctx.psum(dnb)[64:128, :],
                                  cst['onesr_f'][:, 0:64],
                                  rb_row, start=True, stop=True)
                    sBCp.inc(mm)
                bc_mark = sBCp.n

                @block.vector
                def _(v, hp=hp, bcm=bc_mark, avm=av_mark, tq=tq, avb=avb,
                      dnb=dnb):
                    v.wait_ge(sBCp.h, bcm)
                    v.wait_ge(sAV.h, avm)
                    sv.wait(v)
                    sv.inc(v.tensor_copy(rb_sb[:], ctx.psum(dnb)[:, :]))
                    sv.wait(v)
                    sv.inc(v.tensor_tensor(ysb[:, hp, tq], ctx.psum(avb)[:, :],
                                           rb_sb[:], op=ALU.mult))
                avmarks.append(sv.n)

            avmarks = []
            tail_state = {}
            for step in range(NIT + 1):
                if step >= 1:
                    tail_a(step - 1)
                if step < NIT:
                    sc_part(step)
                if step >= 1:
                    tail_b(step - 1)

            att_done = sv.n

            @block.sync
            def _(sy, s_=s_):
                sy.wait_ge(sQKV.h, sQKV.n)
                sy.wait_ge(ctx.sem("p0_Cast").h, ctx.cast_marks[f'attw{l}'])
                sW.inc(sy.dma_start(
                    out=wbuf[:],
                    in_=attw_bf[l, 3].rearrange("(k p) o -> p k o", p=128)), 16)
            wo_mark = sW.n

            for dc in range(KC):
                for ti in range(4):
                    gi = len(opmarks)
                    pb = gi % 2
                    tsl = slice(s_ * SEQ + ti * 512, s_ * SEQ + (ti + 1) * 512)

                    @block.sync
                    def _(sy, dc=dc, pb=pb, gi=gi, tsl=tsl):
                        if gi >= 2:
                            sy.wait_ge(sv.h, opmarks[gi - 2])
                            sy.wait_ge(sYS.h, ys0 + 16 * (gi - 1))
                        sYL.inc(sy.dma_start(out=yres[pb][:],
                                             in_=fmview(y_in)[:, dc, tsl]), 16)
                    yl_mark = sYL.n

                    @block.tensor
                    def _(t, dc=dc, ti=ti, pb=pb, wm=wo_mark, adm=att_done,
                          gi=gi):
                        t.wait_ge(sW.h, wm)
                        t.wait_ge(sv.h, adm)
                        if gi >= 2:
                            t.wait_ge(sv.h, opmarks[gi - 2])
                        ps = ctx.psum(3 + pb)
                        mm = None
                        for k in range(KC):
                            mm = t.matmul(ps[:, :],
                                          wbuf[:, k, dc * 128:(dc + 1) * 128],
                                          ysb[:, k, ti * 512:(ti + 1) * 512],
                                          start=(k == 0), stop=(k == KC - 1))
                        sOP.inc(mm)
                    op_mark = sOP.n

                    @block.scalar
                    def _(s, dc=dc, pb=pb, opm=op_mark):
                        s.wait_ge(sOP.h, opm)
                        s.wait_ge(sv.h, bfin_mark)
                        sa.wait(s)
                        sa.inc(s.activation(ytmp[pb][:], ctx.psum(3 + pb)[:, :],
                                            AF.Identity, bias=bfin[:, dc:dc + 1]))
                    ev_mark = sa.n

                    @block.vector
                    def _(v, pb=pb, evm=ev_mark, ylm=yl_mark):
                        v.wait_ge(sa.h, evm)
                        v.wait_ge(sYL.h, ylm)
                        sv.wait(v)
                        sv.inc(v.tensor_tensor(ytmp[pb][:], ytmp[pb][:],
                                               yres[pb][:], op=ALU.add))
                    opmarks.append(sv.n)

                    @block.sync
                    def _(sy, dc=dc, pb=pb, gi=gi, tsl=tsl):
                        sy.wait_ge(sv.h, opmarks[gi])
                        sYS.inc(sy.dma_start(out=fmview(y_out)[:, dc, tsl],
                                             in_=ytmp[pb][:]), 16)

        @block.sync
        def _(sy):
            sy.wait_ge(sYS.h, sYS.n)
    return ctx


def emit_ln2gate(ctx, l, yin):
    nc = ctx.nc
    y_in = ctx.internal(yin, [D, T], F32)
    gatew = ctx.r("gatew", [NL, D, NEXP], F32)
    gate_b = ctx.r("gate_b", [NEXP, NL], F32)
    hb = ctx.internal("hb", [D, T], BF16)
    token = (l % 2 == 0)
    if token:
        cwT_f = ctx.internal("cwT_f", [NEXP, T], F32)
        cwT_b = ctx.internal("cwT_b", [NEXP, T], BF16)
    else:
        cwsm = ctx.internal("cwsm", [2, NEXP], F32)
        cwsmT = ctx.internal("cwsmT", [NEXP, 2], F32)

    if True:
        block = ctx.block
        cst = ctx.consts
        vs = ctx.vecs
        sv, sa = ctx.sv, ctx.sa
        identF, idf_mark = emit_identF(ctx)

        gatew_sb = ctx.buf("gw_sb", [128, KC * NEXP], F32)
        gb_sb = ctx.buf("gb_sb", [NEXP, NL], F32)
        logitsT = ctx.buf("lgT", [NEXP, T], F32)
        sGW = ctx.sem("gw_L")

        @block.sync
        def _(sy):
            sGW.inc(sy.dma_start(
                out=gatew_sb[:].rearrange("p (c e) -> p c e", c=KC),
                in_=gatew[l].rearrange("(c p) e -> p c e", p=128)), 16)
            sGW.inc(sy.dma_start(out=gb_sb[:], in_=gate_b[:]), 16)

        @block.tensor
        def _(t):
            t.wait_ge(sGW.h, sGW.n)

        lnm = emit_ln(ctx, "ln2", src=y_in, out_bf16=hb, vs=vs,
                      wname='ln2_w', bname='ln2_b', lidx=l,
                      gate=('token' if token else 'seq', l),
                      gatew_sb=gatew_sb, gateb_col=gb_sb[:, l:l + 1],
                      logitsT_sb=logitsT)

        if token:
            lg_tm = ctx.buf("lg_tm", [128, 32 * NEXP], F32)
            cw_tm = ctx.buf("cw_tm", [128, 32 * NEXP], F32)
            m1 = ctx.buf("cw_m1", [128, 32], F32)
            m2 = ctx.buf("cw_m2", [128, 32], F32)
            nm1 = ctx.buf("cw_nm1", [128, 32], F32)
            rr = ctx.buf("cw_rr", [128, 32], F32)
            eqb = ctx.buf("cw_eq", [128, 32 * NEXP], F32)
            cwT_sb = ctx.buf("cwT_sb", [NEXP, T], F32)
            cwTb_sb = ctx.buf("cwTb_sb", [NEXP, T], BF16)
            sTR = ctx.sem("cw_TR")
            sTR2 = ctx.sem("cw_TR2")
            sCWS = ctx.sem("cw_S")
            lgv = lg_tm[:].rearrange("p (r e) -> p r e", r=32)
            cwv = cw_tm[:].rearrange("p (r e) -> p r e", r=32)
            eqv = eqb[:].rearrange("p (r e) -> p r e", r=32)

            evm = []
            for tr in range(32):
                @block.tensor
                def _(t, tr=tr):
                    t.wait_ge(sv.h, max(lnm['lgdone'], idf_mark))
                    if tr >= 2:
                        t.wait_ge(sv.h, evm[tr - 2])
                    mm = t.transpose(ctx.psum(tr % 2)[:, 0:8],
                                     logitsT[0:8, tr * 128:(tr + 1) * 128],
                                     identF[0:8, 0:8])
                    sTR.inc(mm)
                trm = sTR.n

                @block.vector
                def _(v, tr=tr, trm=trm):
                    v.wait_ge(sTR.h, trm)
                    sv.wait(v)
                    sv.inc(v.tensor_copy(lgv[:, tr, :], ctx.psum(tr % 2)[:, 0:8]))
                evm.append(sv.n)

            @block.vector
            def _(v):
                sv.wait(v)
                sv.inc(v.tensor_reduce(m1[:], lgv, axis=AX.X, op=ALU.max))
                sv.wait(v)
                ed = None
                for tr in range(32):
                    ed = v.tensor_scalar(eqv[:, tr, :], lgv[:, tr, :],
                                         m1[:, tr:tr + 1], None, op0=ALU.is_equal)
                sv.inc(ed)
                sv.wait(v)
                sv.inc(v.scalar_tensor_tensor(eqb[:], eqb[:], -1e9, lg_tm[:],
                                              op0=ALU.mult, op1=ALU.add))
                sv.wait(v)
                sv.inc(v.tensor_reduce(m2[:], eqv, axis=AX.X, op=ALU.max))
                sv.wait(v)
                ed = None
                for tr in range(32):
                    ed = v.tensor_scalar(eqv[:, tr, :], lgv[:, tr, :],
                                         m2[:, tr:tr + 1], None, op0=ALU.is_ge)
                sv.inc(ed)
                sv.wait(v)
                sv.inc(v.tensor_scalar(nm1[:], m1[:], -1.0, None, op0=ALU.mult))
                sv.wait(v)
                sv.inc(v.tensor_tensor(m2[:], m2[:], m1[:], op=ALU.subtract))
            pre_mark = sv.n

            @block.scalar
            def _(s):
                s.wait_ge(sv.h, pre_mark)
                sa.wait(s)
                ed = None
                for tr in range(32):
                    ed = s.activation(cwv[:, tr, :], lgv[:, tr, :], AF.Exp,
                                      bias=nm1[:, tr:tr + 1])
                sa.inc(ed)
                sa.wait(s)
                sa.inc(s.activation(m2[:], m2[:], AF.Exp))
            exp_mark = sa.n

            @block.vector
            def _(v):
                v.wait_ge(sa.h, exp_mark)
                sv.wait(v)
                sv.inc(v.tensor_scalar(m2[:], m2[:], 1.0, None, op0=ALU.add))
                sv.wait(v)
                sv.inc(v.reciprocal(rr[:], m2[:]))
                sv.wait(v)
                ed = None
                for tr in range(32):
                    ed = v.scalar_tensor_tensor(cwv[:, tr, :], cwv[:, tr, :],
                                                rr[:, tr:tr + 1], eqv[:, tr, :],
                                                op0=ALU.mult, op1=ALU.mult)
                sv.inc(ed)
            cw_mark = sv.n

            evm2 = []
            for tr in range(32):
                @block.tensor
                def _(t, tr=tr):
                    t.wait_ge(sv.h, cw_mark)
                    if tr >= 2:
                        t.wait_ge(sv.h, evm2[tr - 2])
                    mm = t.transpose(ctx.psum(2 + tr % 2)[0:8, 0:128],
                                     cwv[:, tr, :], identF[:])
                    sTR2.inc(mm)
                trm = sTR2.n

                @block.vector
                def _(v, tr=tr, trm=trm):
                    v.wait_ge(sTR2.h, trm)
                    sv.wait(v)
                    v.tensor_copy(cwT_sb[0:8, tr * 128:(tr + 1) * 128],
                                  ctx.psum(2 + tr % 2)[0:8, 0:128])
                    sv.inc(v.tensor_copy(cwTb_sb[0:8, tr * 128:(tr + 1) * 128],
                                         ctx.psum(2 + tr % 2)[0:8, 0:128]))
                evm2.append(sv.n)

            @block.sync
            def _(sy):
                sy.wait_ge(sv.h, evm2[-1])
                sCWS.inc(sy.dma_start(out=cwT_f[:], in_=cwT_sb[:]), 16)
                sCWS.inc(sy.dma_start(out=cwT_b[:], in_=cwTb_sb[:]), 16)
                sy.wait_ge(sCWS.h, 32)
                sy.wait_ge(ctx.sem("ln_Out").h, ctx.sem("ln_Out").n)
        else:
            hmean = lnm['hmean']
            lgp = ctx.buf("lgp_sb", [NEXP, 2], F32)
            lg2 = ctx.buf("lg2_sb", [2, NEXP], F32)
            cwsm_sb = ctx.buf("cwsm_sb", [2, NEXP], F32)
            cwsmT_sb = ctx.buf("cwsmT_sb", [NEXP, 2], F32)
            m1 = ctx.buf("sq_m1", [2, 1], F32)
            m2 = ctx.buf("sq_m2", [2, 1], F32)
            nm1 = ctx.buf("sq_nm1", [2, 1], F32)
            rr = ctx.buf("sq_rr", [2, 1], F32)
            eqb = ctx.buf("sq_eq", [2, NEXP], F32)
            sLG2 = ctx.sem("sq_LG")
            sTR = ctx.sem("sq_TR")
            sCWS = ctx.sem("sq_S")

            @block.tensor
            def _(t):
                t.wait_ge(sv.h, lnm['hmdone'])
                mm = None
                for c in range(KC):
                    mm = t.matmul(ctx.psum(6)[0:8, 0:2],
                                  gatew_sb[:, c * 8:(c + 1) * 8],
                                  hmean[:, c * 2:(c + 1) * 2],
                                  start=(c == 0), stop=(c == KC - 1))
                sLG2.inc(mm)

            @block.vector
            def _(v):
                v.wait_ge(sLG2.h, sLG2.n)
                sv.wait(v)
                sv.inc(v.tensor_scalar(lgp[:], ctx.psum(6)[0:8, 0:2],
                                       gb_sb[:, l:l + 1], None, op0=ALU.add))
            lgp_mark = sv.n

            @block.tensor
            def _(t):
                t.wait_ge(sv.h, max(lgp_mark, idf_mark))
                mm = t.transpose(ctx.psum(7)[0:2, 0:8], lgp[:], identF[0:8, 0:8])
                sTR.inc(mm)

            @block.vector
            def _(v):
                v.wait_ge(sTR.h, sTR.n)
                sv.wait(v)
                sv.inc(v.tensor_copy(lg2[:], ctx.psum(7)[0:2, 0:8]))
                sv.wait(v)
                sv.inc(v.tensor_reduce(m1[:], lg2[:], axis=AX.X, op=ALU.max))
                sv.wait(v)
                sv.inc(v.tensor_scalar(eqb[:], lg2[:], m1[:], None,
                                       op0=ALU.is_equal))
                sv.wait(v)
                sv.inc(v.scalar_tensor_tensor(eqb[:], eqb[:], -1e9, lg2[:],
                                              op0=ALU.mult, op1=ALU.add))
                sv.wait(v)
                sv.inc(v.tensor_reduce(m2[:], eqb[:], axis=AX.X, op=ALU.max))
                sv.wait(v)
                sv.inc(v.tensor_scalar(eqb[:], lg2[:], m2[:], None, op0=ALU.is_ge))
                sv.wait(v)
                sv.inc(v.tensor_scalar(nm1[:], m1[:], -1.0, None, op0=ALU.mult))
                sv.wait(v)
                sv.inc(v.tensor_tensor(m2[:], m2[:], m1[:], op=ALU.subtract))
            pre_mark = sv.n

            @block.scalar
            def _(s):
                s.wait_ge(sv.h, pre_mark)
                sa.wait(s)
                sa.inc(s.activation(cwsm_sb[:], lg2[:], AF.Exp, bias=nm1[:]))
                sa.wait(s)
                sa.inc(s.activation(m2[:], m2[:], AF.Exp))
            exp_mark = sa.n

            @block.vector
            def _(v):
                v.wait_ge(sa.h, exp_mark)
                sv.wait(v)
                sv.inc(v.tensor_scalar(m2[:], m2[:], 1.0, None, op0=ALU.add))
                sv.wait(v)
                sv.inc(v.reciprocal(rr[:], m2[:]))
                sv.wait(v)
                sv.inc(v.scalar_tensor_tensor(cwsm_sb[:], cwsm_sb[:], rr[:],
                                              eqb[:], op0=ALU.mult, op1=ALU.mult))
            cw_mark = sv.n

            @block.tensor
            def _(t):
                t.wait_ge(sv.h, cw_mark)
                mm = t.transpose(ctx.psum(7)[0:8, 0:2], cwsm_sb[:],
                                 identF[0:2, 0:2])
                sTR.inc(mm)
            tr2 = sTR.n

            @block.vector
            def _(v):
                v.wait_ge(sTR.h, tr2)
                sv.wait(v)
                sv.inc(v.tensor_copy(cwsmT_sb[:], ctx.psum(7)[0:8, 0:2]))
            trc = sv.n

            @block.sync
            def _(sy):
                sy.wait_ge(sv.h, trc)
                sCWS.inc(sy.dma_start(out=cwsm[:], in_=cwsm_sb[:]), 16)
                sCWS.inc(sy.dma_start(out=cwsmT[:], in_=cwsmT_sb[:]), 16)
                sy.wait_ge(sCWS.h, 32)
                sy.wait_ge(ctx.sem("ln_Out").h, ctx.sem("ln_Out").n)
    return ctx


def emit_moe(ctx, l, yin, yout):
    nc = ctx.nc
    token = (l % 2 == 0)
    hb = ctx.internal("hb", [D, T], BF16)
    y_in = ctx.internal(yin, [D, T], F32)
    e_w1bf = ctx.internal("e_w1bf", [NL, NEXP, D, DFF], BF16)
    e_w2bf = ctx.internal("e_w2bf", [NL, NEXP, DFF, D], BF16)
    b2e = ctx.r("b2e", [NL, NEXP, D], F32)
    if token:
        cwT_f = ctx.internal("cwT_f", [NEXP, T], F32)
    else:
        cwsm = ctx.internal("cwsm", [2, NEXP], F32)
        cwsmT = ctx.internal("cwsmT", [NEXP, 2], F32)
    y_out = ctx.internal(yout, [D, T], F32)

    if True:
        block = ctx.block
        cst = ctx.consts
        vs = ctx.vecs
        sv, sa = ctx.sv, ctx.sa
        onehotF, ohf_mark = emit_onehot_r(ctx, 8, 'e')
        if not token:
            rowsel, rs_mark = emit_onehot_r(ctx, 2, 'r2')

        hg = ctx.buf("moe_h", [128, KC, GRP], BF16)
        a1g = ctx.buf("moe_a1g", [128, MC, GRP], BF16)
        acc = ctx.buf("moe_acc", [128, KC, GRP], F32)
        w1s = [ctx.buf(f"moe_w1{i}", [128, KC, 512], BF16) for i in range(2)]
        w2s = [ctx.buf(f"moe_w2{i}", [128, MC, 128], BF16) for i in range(2)]
        tmp = [ctx.buf(f"moe_tmp{i}", [128, 512], BF16) for i in range(8)]
        cwb = ctx.buf("moe_cwb", [128, NEXP * GRP], BF16)
        yrt = [ctx.buf(f"moe_yr{i}", [128, 512], F32) for i in range(2)]
        yot = [ctx.buf(f"moe_yo{i}", [128, 512], F32) for i in range(2)]
        b2sl = ctx.buf("moe_b2", [NEXP, D], F32)
        if token:
            cwTg = ctx.buf("moe_cwTg", [NEXP, GRP], F32)
        else:
            cwsm_sb = ctx.buf("moe_cwsm", [2, NEXP], F32)
            cwsmT_sb = ctx.buf("moe_cwsmT", [NEXP, 2], F32)
            cwcol = ctx.buf("moe_cwcol", [128, NSEQ * NEXP], F32)
            bias2 = ctx.buf("moe_bias2", [128, KC * NSEQ], F32)

        sIn = ctx.sem("moe_In")
        sH = ctx.sem("moe_H")
        sW1 = ctx.sem("moe_W1")
        sW2 = ctx.sem("moe_W2")
        sP1 = ctx.sem("moe_P1")
        sP2 = ctx.sem("moe_P2")
        sCB = ctx.sem("moe_CB")
        sB2 = ctx.sem("moe_B2")
        sYL = ctx.sem("moe_YL")
        sYS = ctx.sem("moe_YS")
        p10 = sP1.n
        p20 = sP2.n
        mys0 = sYS.n

        @block.sync
        def _(sy):
            sIn.inc(sy.dma_start(out=b2sl[:], in_=b2e[l]), 16)
            if token:
                pass
            else:
                sIn.inc(sy.dma_start(out=cwsm_sb[:], in_=cwsm[:]), 16)
                sIn.inc(sy.dma_start(out=cwsmT_sb[:], in_=cwsmT[:]), 16)

        if not token:
            for s_ in range(2):
                for e in range(NEXP):
                    @block.tensor
                    def _(t, s_=s_, e=e):
                        t.wait_ge(sIn.h, sIn.n)
                        t.wait_ge(sv.h, rs_mark)
                        mm = t.matmul(ctx.psum(6)[:, s_ * 8 + e:s_ * 8 + e + 1],
                                      rowsel[:, s_ * 128:(s_ + 1) * 128],
                                      cwsm_sb[0:2, e:e + 1],
                                      start=True, stop=True)
                        sCB.inc(mm)

            @block.vector
            def _(v):
                v.wait_ge(sCB.h, sCB.n)
                sv.wait(v)
                sv.inc(v.tensor_copy(cwcol[:], ctx.psum(6)[:, 0:16]))
            cwcol_mark = sv.n

            for dc in range(KC):
                @block.tensor
                def _(t, dc=dc):
                    t.wait_ge(sIn.h, sIn.n)
                    mm = t.matmul(ctx.psum(7)[:, dc * 2:dc * 2 + 2],
                                  b2sl[0:8, dc * 128:(dc + 1) * 128],
                                  cwsmT_sb[0:8, :], start=True, stop=True)
                    sB2.inc(mm)

            @block.vector
            def _(v):
                v.wait_ge(sB2.h, sB2.n)
                sv.wait(v)
                sv.inc(v.tensor_copy(bias2[:], ctx.psum(7)[:, 0:16]))
            bias2_mark = sv.n

        sa_base = sa.n
        p1marks = []
        a1marks = []
        p2marks = []
        ysmarks = []

        for g in range(NGRP):
            gsl = slice(g * GRP, (g + 1) * GRP)
            s_of_g = (g * GRP) // SEQ

            @block.sync
            def _(sy, g=g, gsl=gsl):
                if g > 0:
                    sy.wait_ge(sP1.h, p10 + g * NEXP * MC * NTG)
                sH.inc(sy.dma_start(out=hg[:], in_=fmview(hb)[:, :, gsl]), 16)
                if token:
                    if g > 0:
                        sy.wait_ge(sB2.h, sB2.n)
                    sH.inc(sy.dma_start(out=cwTg[:], in_=cwT_f[:, gsl]), 16)
            h_mark = sH.n

            if token:
                for e in range(NEXP):
                    for tt in range(NTG):
                        @block.tensor
                        def _(t, g=g, e=e, tt=tt):
                            t.wait_ge(sIn.h, sIn.n)
                            t.wait_ge(sv.h, ohf_mark)
                            if g > 0 and e == 0 and tt == 0:
                                t.wait_ge(sa.h, p1marks[-1])
                                t.wait_ge(sv.h, ysmarks[-1] if ysmarks else 0)
                            t.wait_ge(sH.h, h_mark)
                            mm = t.matmul(
                                ctx.psum(tt % 2)[:, :],
                                onehotF[:, e * 128:(e + 1) * 128],
                                cwTg[0:8, tt * 512:(tt + 1) * 512],
                                start=True, stop=True)
                            sCB.inc(mm)
                        cbm = sCB.n

                        @block.vector
                        def _(v, e=e, tt=tt, cbm=cbm):
                            v.wait_ge(sCB.h, cbm)
                            sv.wait(v)
                            sv.inc(v.tensor_copy(
                                cwb[:, e * GRP + tt * 512:e * GRP + (tt + 1) * 512],
                                ctx.psum(tt % 2)[:, :]))
                cwb_mark = sv.n

            for e in range(NEXP):
                for blk in range(DFF // 512):
                    giw = g * NEXP * (DFF // 512) + e * (DFF // 512) + blk

                    @block.sync
                    def _(sy, e=e, blk=blk, giw=giw, g=g):
                        if giw >= 2:
                            sy.wait_ge(sP1.h, p10 + (giw - 1) * 4 * NTG)
                        elif giw == 0:
                            sy.wait_ge(ctx.sem("p0_Cast").h,
                                       ctx.cast_marks[f'ew{l}'])
                        src_ = e_w1bf[l, e].rearrange("(k p) n -> p k n", p=128)
                        sW1.inc(sy.dma_start(
                            out=w1s[giw % 2][:],
                            in_=src_[:, :, blk * 512:(blk + 1) * 512]), 16)
                    w1_mark = sW1.n

                    gmt0 = len(p1marks)
                    p1m_grp = []

                    @block.tensor
                    def _(t, g=g, e=e, blk=blk, gmt0=gmt0, w1m=w1_mark,
                          hm=h_mark, giw=giw):
                        t.wait_ge(sW1.h, w1m)
                        t.wait_ge(sH.h, hm)
                        if token and e == 0 and blk == 0:
                            t.wait_ge(sv.h, cwb_mark)
                        if (e > 0 or g > 0) and blk == 0:
                            t.wait_ge(sv.h, p2marks[-1] if p2marks else 0)
                        for mloc in range(4):
                            m = blk * 4 + mloc
                            for tt in range(NTG):
                                gmt = gmt0 + mloc * NTG + tt
                                if gmt >= 4:
                                    t.wait_ge(sa.h, sa_base + gmt - 3)
                                ps = ctx.psum((m * NTG + tt) % 4)
                                mm = None
                                for k in range(KC):
                                    mm = t.matmul(
                                        ps[:, :],
                                        w1s[giw % 2][:, k,
                                                     mloc * 128:(mloc + 1) * 128],
                                        hg[:, k, tt * 512:(tt + 1) * 512],
                                        start=(k == 0), stop=(k == KC - 1))
                                sP1.inc(mm)
                                p1m_grp.append(sP1.n)

                    @block.scalar
                    def _(s, l=l, e=e, blk=blk, gmt0=gmt0):
                        for mloc in range(4):
                            m = blk * 4 + mloc
                            for tt in range(NTG):
                                j = mloc * NTG + tt
                                gmt = gmt0 + j
                                s.wait_ge(sP1.h, p1m_grp[j])
                                if gmt >= 8:
                                    s.wait_ge(sv.h, a1marks[gmt - 8])
                                bcol = VO['e_b1'] + (l * NEXP + e) * MC + m
                                sa.inc(s.activation(tmp[gmt % 8][:],
                                                    ctx.psum((m * NTG + tt) % 4)[:, :],
                                                    AF.Gelu,
                                                    bias=vs[:, bcol:bcol + 1]))
                                p1marks.append(sa.n)
                                assert sa.n == sa_base + gmt + 1

                    @block.vector
                    def _(v, g=g, e=e, blk=blk, gmt0=gmt0):
                        for mloc in range(4):
                            m = blk * 4 + mloc
                            for tt in range(NTG):
                                gmt = gmt0 + mloc * NTG + tt
                                v.wait_ge(sa.h, p1marks[gmt])
                                sv.wait(v)
                                if token:
                                    ed = v.tensor_tensor(
                                        a1g[:, m, tt * 512:(tt + 1) * 512],
                                        tmp[gmt % 8][:],
                                        cwb[:, e * GRP + tt * 512:
                                            e * GRP + (tt + 1) * 512],
                                        op=ALU.mult)
                                else:
                                    s_ = (g * GRP) // SEQ
                                    ed = v.tensor_scalar(
                                        a1g[:, m, tt * 512:(tt + 1) * 512],
                                        tmp[gmt % 8][:],
                                        cwcol[:, s_ * 8 + e:s_ * 8 + e + 1], None,
                                        op0=ALU.mult)
                                sv.inc(ed)
                                a1marks.append(sv.n)

                for d in range(KC):
                    gid = g * NEXP * KC + e * KC + d

                    @block.sync
                    def _(sy, e=e, d=d, gid=gid):
                        if gid >= 2:
                            sy.wait_ge(sP2.h, p20 + (gid - 1) * NTG)
                        elif gid == 0:
                            sy.wait_ge(ctx.sem("p0_Cast").h,
                                       ctx.cast_marks[f'ew{l}'])
                        src_ = e_w2bf[l, e].rearrange("(m p) n -> p m n", p=128)
                        sW2.inc(sy.dma_start(
                            out=w2s[gid % 2][:],
                            in_=src_[:, :, d * 128:(d + 1) * 128]), 16)
                    w2_mark = sW2.n

                    for tt in range(NTG):
                        gdt = len(p2marks)
                        pbank = 4 + (d * NTG + tt) % 4

                        @block.tensor
                        def _(t, e=e, d=d, tt=tt, gdt=gdt, pbank=pbank,
                              w2m=w2_mark, gid=gid):
                            t.wait_ge(sW2.h, w2m)
                            if d == 0 and tt == 0:
                                t.wait_ge(sv.h, a1marks[-1])
                            if gdt >= 4:
                                t.wait_ge(sv.h, p2marks[gdt - 4])
                            elif not token:
                                t.wait_ge(sv.h, bias2_mark)
                            ps = ctx.psum(pbank)
                            mm = None
                            for m in range(MC):
                                mm = t.matmul(
                                    ps[:, :], w2s[gid % 2][:, m, :],
                                    a1g[:, m, tt * 512:(tt + 1) * 512],
                                    start=(m == 0), stop=(m == MC - 1))
                            sP2.inc(mm)
                        p2m = sP2.n

                        @block.vector
                        def _(v, e=e, d=d, tt=tt, gdt=gdt, pbank=pbank, p2m=p2m,
                              g=g):
                            v.wait_ge(sP2.h, p2m)
                            sv.wait(v)
                            if e == 0:
                                ed = v.tensor_copy(
                                    acc[:, d, tt * 512:(tt + 1) * 512],
                                    ctx.psum(pbank)[:, :])
                            else:
                                ed = v.tensor_tensor(
                                    acc[:, d, tt * 512:(tt + 1) * 512],
                                    acc[:, d, tt * 512:(tt + 1) * 512],
                                    ctx.psum(pbank)[:, :], op=ALU.add)
                            sv.inc(ed)
                        p2marks.append(sv.n)

            for d in range(KC):
                for tt in range(NTG):
                    gy = len(ysmarks)
                    pb = gy % 2
                    tsl = slice(g * GRP + tt * 512, g * GRP + (tt + 1) * 512)

                    @block.sync
                    def _(sy, d=d, pb=pb, gy=gy, tsl=tsl):
                        if gy >= 2:
                            sy.wait_ge(sv.h, ysmarks[gy - 2])
                            sy.wait_ge(sYS.h, mys0 + 16 * (gy - 1))
                        sYL.inc(sy.dma_start(out=yrt[pb][:],
                                             in_=fmview(y_in)[:, d, tsl]), 16)
                    yl_mark = sYL.n

                    if token:
                        @block.tensor
                        def _(t, d=d, tt=tt, g=g, gy=gy):
                            t.wait_ge(sv.h, p2marks[-1])
                            t.wait_ge(sIn.h, sIn.n)
                            if gy >= 2:
                                t.wait_ge(sv.h, ysmarks[gy - 2])
                            mm = t.matmul(
                                ctx.psum(gy % 2)[:, :],
                                b2sl[0:8, d * 128:(d + 1) * 128],
                                cwTg[0:8, tt * 512:(tt + 1) * 512],
                                start=True, stop=True)
                            sB2.inc(mm)
                        b2m = sB2.n

                    @block.vector
                    def _(v, d=d, tt=tt, g=g, gy=gy, pb=pb, ylm=yl_mark):
                        v.wait_ge(sYL.h, ylm)
                        v.wait_ge(sv.h, p2marks[-1])
                        if token:
                            v.wait_ge(sB2.h, sB2.n)
                        else:
                            v.wait_ge(sv.h, bias2_mark)
                        sv.wait(v)
                        ed = v.tensor_tensor(yot[pb][:], yrt[pb][:],
                                             acc[:, d, tt * 512:(tt + 1) * 512],
                                             op=ALU.add)
                        sv.inc(ed)
                        sv.wait(v)
                        if token:
                            ed = v.tensor_tensor(yot[pb][:], yot[pb][:],
                                                 ctx.psum(gy % 2)[:, :],
                                                 op=ALU.add)
                        else:
                            s_ = (g * GRP) // SEQ
                            ed = v.tensor_scalar(yot[pb][:], yot[pb][:],
                                                 bias2[:, d * 2 + s_:d * 2 + s_ + 1],
                                                 None, op0=ALU.add)
                        sv.inc(ed)
                    ysmarks.append(sv.n)

                    @block.sync
                    def _(sy, d=d, pb=pb, gy=gy, tsl=tsl):
                        sy.wait_ge(sv.h, ysmarks[gy])
                        sYS.inc(sy.dma_start(out=fmview(y_out)[:, d, tsl],
                                             in_=yot[pb][:]), 16)

        @block.sync
        def _(sy):
            sy.wait_ge(sYS.h, sYS.n)
    return ctx


def emit_p45(ctx):
    nc = ctx.nc
    ctx.r("x_fm", [D, T], F32)
    ctx.internal("s0", [D, T], F32)
    ctx.internal("yffn1", [D, T], F32)
    xn1f = ctx.internal("xn1f", [D, T], F32)
    ctx.internal("x1", [D, T], F32)
    cc1_in = ctx.internal("cc1_in", [1, 1], F32)
    cc1_out = ctx.internal("cc1_out", [1, 1], F32)

    if True:
        block = ctx.block
        sv, sg = ctx.sv, ctx.sg
        am = ctx.buf("p45_am", [128, 1], F32)
        am1 = ctx.buf("p45_am1", [1, 1], F32)
        lnm = emit_ln(ctx, "ln45", out_f32=xn1f,
                      absmax_sb=am[:], prologue='x1')
        sCC = ctx.sem("p45_CC")
        sCR = ctx.sem("p45_CR")

        @block.gpsimd
        def _(g):
            g.wait_ge(sv.h, lnm['amdone'])
            sg.inc(g.tensor_reduce(am1[:], am[:], axis=AX.C, op=ALU.max))
            g.wait_ge(sg.h, sg.n)
            sCC.inc(g.dma_start(out=cc1_in[:], in_=am1[:]), 16)
            g.wait_ge(sCC.h, sCC.n)
            g.collective_compute(
                "AllReduce", ALU.max,
                replica_groups=[list(range(NCORES))],
                ins=[cc1_in.opt()], outs=[cc1_out.opt()],
            ).then_inc(sCR.h, 1)
            sCR.n = max(sCR.n, 1)

        @block.sync
        def _(sy):
            sy.wait_ge(ctx.sem("ln_Out").h, ctx.sem("ln_Out").n)
            sy.wait_ge(sCC.h, sCC.n)
    return ctx


def emit_z1(ctx):
    nc = ctx.nc
    xn1f = ctx.internal("xn1f", [D, T], F32)
    z1b = ctx.internal("z1b", [D, T], BF16)
    if True:
        block = ctx.block
        emit_ln(ctx, "lnz1", src=xn1f, out_bf16=z1b)

        @block.sync
        def _(sy):
            sy.wait_ge(ctx.sem("ln_Out").h, ctx.sem("ln_Out").n)
    return ctx


def emit_p6(ctx):
    nc = ctx.nc
    z1b = ctx.internal("z1b", [D, T], BF16)
    W1T_bf = ctx.internal("W1T_bf", [D, 3 * D], BF16)
    W2T_bf = ctx.internal("W2T_bf", [D, D], BF16)
    s1 = ctx.internal("s1", [D, T], F32)

    if True:
        block = ctx.block
        cst = ctx.consts
        sv, sa = ctx.sv, ctx.sa
        zseq = ctx.buf("p6_z", [128, KC, SEQ], BF16)
        ysb = zseq
        qsb = ctx.buf("p6_q", [128, KC, SEQ], BF16)
        ksb = ctx.buf("p6_k", [128, KC, SEQ], BF16)
        vtm = ctx.buf("p6_v", [128, 16, 1024], BF16)
        wbuf = ctx.buf("p6_w", [128, KC, D], BF16)
        ea = ctx.buf("p6_ea", [128, 16, 512], BF16)
        eb = ctx.buf("p6_eb", [128, 16, 512], BF16)
        rb_sb = ctx.buf("p6_rb", [128, 512], F32)
        rrows = ctx.buf("p6_rr", [1, 1024], F32)
        ra_row = rrows[:, 0:512]
        rb_row = rrows[:, 512:1024]
        mneg = [ctx.buf(f"p6_mn{i}", [128, 512], F32) for i in range(4)]
        expb = mneg
        ytmp = [ctx.buf(f"p6_yt{i}", [128, 512], F32) for i in range(2)]

        sZ = ctx.sem("p6_Z")
        sW = ctx.sem("p6_W")
        sQKV = ctx.sem("p6_QKV")
        sSC = ctx.sem("p6_SC")
        sDN = ctx.sem("p6_DN")
        sAV = ctx.sem("p6_AV")
        sBCp = ctx.sem("p6_BC")
        sOP = ctx.sem("p6_OP")
        sYS = ctx.sem("p6_YS")
        evmarks = []
        scmarks = []
        avmarks = []
        opmarks = []

        for s_ in range(NSEQ):
            ssl = slice(s_ * SEQ, (s_ + 1) * SEQ)

            @block.sync
            def _(sy, ssl=ssl, s_=s_):
                if s_ > 0:
                    sy.wait_ge(sOP.h, sOP.n)
                sZ.inc(sy.dma_start(out=zseq[:], in_=fmview(z1b)[:, :, ssl]), 16)
            z_mark = sZ.n

            for wi in range(3):
                @block.sync
                def _(sy, wi=wi, s_=s_):
                    if not (s_ == 0 and wi == 0):
                        sy.wait_ge(sQKV.h, sQKV.n)
                    sy.wait_ge(ctx.sem("p0_Cast").h, ctx.cast_marks['w12'])
                    sW.inc(sy.dma_start(
                        out=wbuf[:],
                        in_=W1T_bf[:, wi * D:(wi + 1) * D].rearrange(
                            "(k p) o -> p k o", p=128)), 16)
                w_mark = sW.n

                if wi < 2:
                    dst = qsb if wi == 0 else ksb
                    for oc in range(KC):
                        for ti in range(4):
                            gi = len(evmarks)
                            pbk = gi % 4

                            @block.tensor
                            def _(t, oc=oc, ti=ti, pbk=pbk, wm=w_mark,
                                  zm=z_mark, gi=gi):
                                t.wait_ge(sW.h, wm)
                                t.wait_ge(sZ.h, zm)
                                if gi >= 4:
                                    t.wait_ge(sv.h, evmarks[gi - 4])
                                ps = ctx.psum(pbk)
                                mm = None
                                for k in range(KC):
                                    mm = t.matmul(
                                        ps[:, :],
                                        wbuf[:, k, oc * 128:(oc + 1) * 128],
                                        zseq[:, k, ti * 512:(ti + 1) * 512],
                                        start=(k == 0), stop=(k == KC - 1))
                                sQKV.inc(mm)
                            q_mark = sQKV.n

                            @block.vector
                            def _(v, pbk=pbk, qm=q_mark, gi=gi):
                                v.wait_ge(sQKV.h, qm)
                                sv.wait(v)
                                sv.inc(v.tensor_scalar(mneg[gi % 4][:],
                                                       ctx.psum(pbk)[:, :],
                                                       0.0, None, op0=ALU.min))
                            mn_mark = sv.n

                            @block.scalar
                            def _(s, gi=gi, mnm=mn_mark):
                                s.wait_ge(sv.h, mnm)
                                sa.wait(s)
                                sa.inc(s.activation(expb[gi % 4][:],
                                                    mneg[gi % 4][:], AF.Exp))
                            ex_mark = sa.n

                            @block.vector
                            def _(v, dst=dst, oc=oc, ti=ti, pbk=pbk, gi=gi,
                                  exm=ex_mark):
                                v.wait_ge(sa.h, exm)
                                sv.wait(v)
                                sv.inc(v.scalar_tensor_tensor(
                                    dst[:, oc, ti * 512:(ti + 1) * 512],
                                    ctx.psum(pbk)[:, :], 0.0, expb[gi % 4][:],
                                    op0=ALU.max, op1=ALU.add))
                            evmarks.append(sv.n)
                else:
                    for tr in range(16):
                        for vd in range(2):
                            gi = len(evmarks)
                            pbk = gi % 4

                            @block.tensor
                            def _(t, tr=tr, vd=vd, pbk=pbk, wm=w_mark,
                                  zm=z_mark, gi=gi):
                                t.wait_ge(sW.h, wm)
                                t.wait_ge(sZ.h, zm)
                                if gi >= 4:
                                    t.wait_ge(sv.h, evmarks[gi - 4])
                                ps = ctx.psum(pbk)
                                mm = None
                                for k in range(KC):
                                    mm = t.matmul(
                                        ps[:, :],
                                        zseq[:, k, tr * 128:(tr + 1) * 128],
                                        wbuf[:, k, vd * 512:(vd + 1) * 512],
                                        start=(k == 0), stop=(k == KC - 1))
                                sQKV.inc(mm)
                            v_mark = sQKV.n

                            @block.vector
                            def _(v, tr=tr, vd=vd, pbk=pbk, vm=v_mark):
                                v.wait_ge(sQKV.h, vm)
                                sv.wait(v)
                                sv.inc(v.tensor_copy(
                                    vtm[:, tr, vd * 512:(vd + 1) * 512],
                                    ctx.psum(pbk)[:, :]))
                            evmarks.append(sv.n)
            qkv_done = sv.n

            for hp in range(KC):
                for tqt in range(4):
                    tq = slice(tqt * 512, (tqt + 1) * 512)
                    ntk = 4 * (tqt + 1)
                    for tk in range(ntk):
                        pba = 4 + (tk % 2)

                        @block.tensor
                        def _(t, hp=hp, tqt=tqt, tk=tk, pba=pba, tq=tq):
                            if hp == 0 and tqt == 0 and tk in (0, 1):
                                t.wait_ge(sv.h, qkv_done)
                            if sSC.n >= 2:
                                t.wait_ge(sv.h, scmarks[sSC.n - 2])
                            pa = ctx.psum(pba)
                            pbnk = ctx.psum(6 + (tk % 2))
                            mm = t.matmul(pa[:, :],
                                          ksb[0:64, hp, tk * 128:(tk + 1) * 128],
                                          qsb[0:64, hp, tq],
                                          start=True, stop=True)
                            mm = t.matmul(pbnk[:, :],
                                          ksb[64:128, hp, tk * 128:(tk + 1) * 128],
                                          qsb[64:128, hp, tq],
                                          start=True, stop=True)
                            sSC.inc(mm)
                        sc_mark = sSC.n

                        @block.vector
                        def _(v, tqt=tqt, tk=tk, pba=pba, scm=sc_mark):
                            v.wait_ge(sSC.h, scm)
                            cst['_sem'].wait(v)
                            sv.wait(v)
                            j = tk - 4 * tqt
                            if j >= 0:
                                v.tensor_tensor(ea[:, tk, :], ctx.psum(pba)[:, :],
                                                cst['masks'][j][:], op=ALU.mult)
                                ed = v.tensor_tensor(eb[:, tk, :],
                                                     ctx.psum(6 + (tk % 2))[:, :],
                                                     cst['masks'][j][:],
                                                     op=ALU.mult)
                            else:
                                v.tensor_copy(ea[:, tk, :], ctx.psum(pba)[:, :])
                                ed = v.tensor_copy(eb[:, tk, :],
                                                   ctx.psum(6 + (tk % 2))[:, :])
                            sv.inc(ed)
                        scmarks.append(sv.n)
                    msk_mark = sv.n

                    it = hp * 4 + tqt
                    avb = 0 if it % 2 == 0 else 2
                    dnb = 1 if it % 2 == 0 else 3

                    @block.tensor
                    def _(t, hp=hp, ntk=ntk, mm_=msk_mark, avb=avb, dnb=dnb):
                        t.wait_ge(sv.h, mm_)
                        cst['_sem'].wait(t)
                        if len(avmarks) >= 2:
                            t.wait_ge(sv.h, avmarks[-2])
                        mm = None
                        for tk in range(ntk):
                            mm = t.matmul(ctx.psum(dnb)[0:1, :], cst['ones_b'][:],
                                          ea[:, tk, :], start=(tk == 0),
                                          stop=(tk == ntk - 1))
                        sDN.inc(mm)
                        for tk in range(ntk):
                            mm = t.matmul(ctx.psum(dnb)[32:33, :], cst['ones_b'][:],
                                          eb[:, tk, :], start=(tk == 0),
                                          stop=(tk == ntk - 1))
                        sDN.inc(mm)
                        for tk in range(ntk):
                            mm = t.matmul(ctx.psum(avb)[0:64, :],
                                          vtm[:, tk, hp * 128:hp * 128 + 64],
                                          ea[:, tk, :], start=(tk == 0),
                                          stop=(tk == ntk - 1))
                        for tk in range(ntk):
                            mm = t.matmul(ctx.psum(avb)[64:128, :],
                                          vtm[:, tk, hp * 128 + 64:(hp + 1) * 128],
                                          eb[:, tk, :], start=(tk == 0),
                                          stop=(tk == ntk - 1))
                        sAV.inc(mm)
                    dn_mark = sDN.n
                    av_mark = sAV.n

                    @block.vector
                    def _(v, dnm=dn_mark, dnb=dnb):
                        v.wait_ge(sDN.h, dnm)
                        sv.wait(v)
                        sv.inc(v.reciprocal(ra_row, ctx.psum(dnb)[0:1, :]))
                        sv.wait(v)
                        sv.inc(v.reciprocal(rb_row, ctx.psum(dnb)[32:33, :]))
                    rc_mark = sv.n

                    @block.tensor
                    def _(t, rcm=rc_mark, dnb=dnb):
                        t.wait_ge(sv.h, rcm)
                        t.matmul(ctx.psum(dnb)[0:64, :], cst['onesr_f'][:, 0:64],
                                 ra_row, start=True, stop=True)
                        mm = t.matmul(ctx.psum(dnb)[64:128, :],
                                      cst['onesr_f'][:, 0:64],
                                      rb_row, start=True, stop=True)
                        sBCp.inc(mm)
                    bc_mark = sBCp.n

                    @block.vector
                    def _(v, hp=hp, bcm=bc_mark, avm=av_mark, tq=tq,
                          avb=avb, dnb=dnb):
                        v.wait_ge(sBCp.h, bcm)
                        v.wait_ge(sAV.h, avm)
                        sv.wait(v)
                        sv.inc(v.tensor_copy(rb_sb[:], ctx.psum(dnb)[:, :]))
                        sv.wait(v)
                        sv.inc(v.tensor_tensor(ysb[:, hp, tq], ctx.psum(avb)[:, :],
                                               rb_sb[:], op=ALU.mult))
                    avmarks.append(sv.n)
            att_done = sv.n

            @block.sync
            def _(sy):
                sy.wait_ge(sQKV.h, sQKV.n)
                sy.wait_ge(ctx.sem("p0_Cast").h, ctx.cast_marks['w12'])
                sW.inc(sy.dma_start(
                    out=wbuf[:],
                    in_=W2T_bf.rearrange("(k p) o -> p k o", p=128)), 16)
            wo_mark = sW.n

            for dc in range(KC):
                for ti in range(4):
                    gi = len(opmarks)
                    pb = gi % 2
                    tsl = slice(s_ * SEQ + ti * 512, s_ * SEQ + (ti + 1) * 512)

                    @block.tensor
                    def _(t, dc=dc, ti=ti, pb=pb, wm=wo_mark, adm=att_done,
                          gi=gi):
                        t.wait_ge(sW.h, wm)
                        t.wait_ge(sv.h, adm)
                        if gi >= 2:
                            t.wait_ge(sv.h, opmarks[gi - 2])
                        ps = ctx.psum(3 + pb)
                        mm = None
                        for k in range(KC):
                            mm = t.matmul(ps[:, :],
                                          wbuf[:, k, dc * 128:(dc + 1) * 128],
                                          ysb[:, k, ti * 512:(ti + 1) * 512],
                                          start=(k == 0), stop=(k == KC - 1))
                        sOP.inc(mm)
                    op_mark = sOP.n

                    @block.vector
                    def _(v, pb=pb, opm=op_mark, gi=gi):
                        v.wait_ge(sOP.h, opm)
                        if gi >= 2:
                            v.wait_ge(sYS.h, 16 * (gi - 1))
                        sv.wait(v)
                        sv.inc(v.tensor_copy(ytmp[pb][:], ctx.psum(3 + pb)[:, :]))
                    opmarks.append(sv.n)

                    @block.sync
                    def _(sy, dc=dc, pb=pb, gi=gi, tsl=tsl):
                        sy.wait_ge(sv.h, opmarks[gi])
                        sYS.inc(sy.dma_start(out=fmview(s1)[:, dc, tsl],
                                             in_=ytmp[pb][:]), 16)

        @block.sync
        def _(sy):
            sy.wait_ge(sYS.h, sYS.n)
    return ctx


def emit_p7(ctx):
    split = None
    nc = ctx.nc
    xn1f = ctx.internal("xn1f", [D, T], F32)
    x1 = ctx.internal("x1", [D, T], F32)
    s1 = ctx.internal("s1", [D, T], F32)
    wgq = ctx.internal("wgq", [IFF, D], BF16)
    wuq = ctx.internal("wuq", [IFF, D], BF16)
    wdq = ctx.internal("wdq", [D, IFF], BF16)
    qscale = ctx.internal("qscale", [128, 2 * ICH + KC], F32)
    clip = ctx.r("clip", [1, 2], F32)
    cc1_out = ctx.internal("cc1_out", [1, 1], F32)
    cc2_in = ctx.internal("cc2_in", [1, 1], F32)
    cc2_out = ctx.internal("cc2_out", [1, 1], F32)
    hq_dram = ctx.internal("hq_dram", [IFF, T], F32)
    out_fm = ctx.w("out_fm", [D, T], F32)

    if True:
        block = ctx.block
        cst = ctx.consts
        sv, sa, sg = ctx.sv, ctx.sa, ctx.sg
        cq = ctx.buf("p7_cq", [128, KC, T], BF16)
        wpool = ctx.buf("p7_wp", [128, 2 * KC * IFF], BF16)
        WG0 = 0
        WU0 = KC * IFF
        qsc = ctx.buf("p7_qsc", [128, 2 * ICH + KC], F32)
        clip_sb = ctx.buf("p7_clip", [1, 2], F32)
        gmax_sb = ctx.buf("p7_gm", [1, 1], F32)
        srow = ctx.buf("p7_srow", [1, 2], F32)
        sxb = ctx.buf("p7_sxb", [128, 2], F32)
        shb = ctx.buf("p7_shb", [128, 2], F32)
        svg = ctx.buf("p7_svg", [128, ICH], F32)
        svu = ctx.buf("p7_svu", [128, ICH], F32)
        svd = ctx.buf("p7_svd", [128, KC], F32)
        _xt = ctx.buf("p7_x0", [128, KC, 512], F32)
        xt = [_xt, _xt]
        gt = [ctx.buf(f"p7_g{i}", [128, 512], F32) for i in range(2)]
        ut = [ctx.buf(f"p7_u{i}", [128, 512], F32) for i in range(2)]
        ht = [ctx.buf(f"p7_h{i}", [128, 512], F32) for i in range(2)]
        hqt = [ctx.buf(f"p7_hq{i}", [128, 512], BF16) for i in range(2)]
        amx2 = ctx.buf("p7_amx", [128, ICH * NTILE], F32)
        am1 = ctx.buf("p7_am1", [1, 1], F32)
        x1t = [ctx.buf(f"p7_x1{i}", [128, 512], F32) for i in range(2)]
        s1t = [ctx.buf(f"p7_s1{i}", [128, 512], F32) for i in range(2)]
        ott = [ctx.buf(f"p7_o{i}", [128, 512], F32) for i in range(2)]

        sIn = ctx.sem("p7_In")
        sXL = ctx.sem("p7_XL")
        sWT = ctx.sem("p7_WT")
        sGU = ctx.sem("p7_GU")
        sHS = ctx.sem("p7_HS")
        sHL = ctx.sem("p7_HL")
        sDM = ctx.sem("p7_DM")
        sCC = ctx.sem("p7_CC")
        sRL = ctx.sem("p7_RL")
        sOS = ctx.sem("p7_OS")
        sCCL = ctx.sem("p7_CCL")
        hqmarks = []
        outmarks = []

        hs_done = 0

        @block.sync
        def _(sy):
            sIn.inc(sy.dma_start(out=qsc[:], in_=qscale[:]), 16)
            sIn.inc(sy.dma_start(out=clip_sb[:], in_=clip[:]), 16)
            if split != 'b':
                for hc in range(KC):
                    sWT.inc(sy.dma_start_transpose(
                        out=wpool[:, WG0 + hc * IFF:WG0 + (hc + 1) * IFF],
                        in_=wgq[:, hc * 128:(hc + 1) * 128]), 16)
                for hc in range(KC):
                    sWT.inc(sy.dma_start_transpose(
                        out=wpool[:, WU0 + hc * IFF:WU0 + (hc + 1) * IFF],
                        in_=wuq[:, hc * 128:(hc + 1) * 128]), 16)
        wt_mark = sWT.n

        if split != 'b':
            @block.gpsimd
            def _(g):
                g.wait_ge(ctx.sem("p45_CR").h, 1)
                sCCL.inc(g.dma_start(out=gmax_sb[:], in_=cc1_out[:]), 16)
        gm_mark = sCCL.n

        if split != 'b':
            @block.scalar
            def _(s):
                s.wait_ge(sIn.h, 32)
                s.wait_ge(sCCL.h, gm_mark)
                sa.wait(s)
                sa.inc(s.activation(srow[:, 0:1], clip_sb[:, 0:1], AF.Sigmoid))
        sig_mark = sa.n

        if split != 'b':
            @block.vector
            def _(v):
                v.wait_ge(sa.h, sig_mark)
                sv.wait(v)
                sv.inc(v.tensor_tensor(srow[:, 0:1], srow[:, 0:1], gmax_sb[:],
                                       op=ALU.mult))
                sv.wait(v)
                sv.inc(v.tensor_scalar(srow[:, 0:1], srow[:, 0:1], 1.0 / 127.0, None,
                                       op0=ALU.mult))
                sv.wait(v)
                sv.inc(v.reciprocal(srow[:, 1:2], srow[:, 0:1]))
            sx_mark = sv.n

            @block.tensor
            def _(t):
                t.wait_ge(sv.h, sx_mark)
                cst['_sem'].wait(t)
                mm = t.matmul(ctx.psum(6)[:, 0:2], cst['onesr_f'][:], srow[:, 0:2],
                              start=True, stop=True)
                ctx.sem("p7_BX").inc(mm)

            @block.vector
            def _(v):
                v.wait_ge(ctx.sem("p7_BX").h, ctx.sem("p7_BX").n)
                sv.wait(v)
                sv.inc(v.tensor_copy(sxb[:], ctx.psum(6)[:, 0:2]))
                sv.wait(v)
                sv.inc(v.tensor_scalar(svg[:], qsc[:, 0:ICH], sxb[:, 0:1], None,
                                       op0=ALU.mult))
                sv.wait(v)
                sv.inc(v.tensor_scalar(svu[:], qsc[:, ICH:2 * ICH], sxb[:, 0:1], None,
                                       op0=ALU.mult))
            sxb_mark = sv.n

            cqmarks = []
            for ti in range(NTILE):
                pb = ti % 2
                tsl = slice(ti * 512, (ti + 1) * 512)

                @block.sync
                def _(sy, ti=ti, pb=pb, tsl=tsl):
                    if ti >= 1:
                        sy.wait_ge(sv.h, cqmarks[ti - 1])
                    sXL.inc(sy.dma_start(out=xt[pb][:],
                                         in_=fmview(xn1f)[:, :, tsl]), 16)
                xl_mark = sXL.n

                @block.vector
                def _(v, ti=ti, pb=pb, tsl=tsl, xlm=xl_mark):
                    v.wait_ge(sXL.h, xlm)
                    v.wait_ge(sv.h, sxb_mark)
                    sv.wait(v)
                    ed = None
                    for c in range(KC):
                        ed = v.tensor_scalar(xt[pb][:, c], xt[pb][:, c], sxb[:, 1:2],
                                             None, op0=ALU.mult)
                    sv.inc(ed)
                    sv.wait(v)
                    ed = None
                    for c in range(KC):
                        ed = v.tensor_scalar(xt[pb][:, c], xt[pb][:, c], RND_C, RND_C,
                                             op0=ALU.add, op1=ALU.subtract)
                    sv.inc(ed)
                    sv.wait(v)
                    ed = None
                    for c in range(KC):
                        ed = v.tensor_scalar(cq[:, c, tsl], xt[pb][:, c],
                                             -128.0, 127.0, op0=ALU.max, op1=ALU.min)
                    sv.inc(ed)
                cqmarks.append(sv.n)
            cq_done = sv.n

            gumarks = []
            humarks = []
            for ic in range(ICH):
                for ti in range(NTILE):
                    gidx = len(gumarks)
                    pb = gidx % 2
                    tsl = slice(ti * 512, (ti + 1) * 512)

                    @block.tensor
                    def _(t, ic=ic, ti=ti, gidx=gidx, tsl=tsl):
                        if gidx == 0:
                            t.wait_ge(sv.h, cq_done)
                            t.wait_ge(sWT.h, wt_mark)
                        if gidx >= 2:
                            t.wait_ge(sv.h, humarks[gidx - 2])
                        pg = ctx.psum(gidx % 2)
                        pu = ctx.psum(2 + gidx % 2)
                        mm = None
                        for k in range(KC):
                            mm = t.matmul(pg[:, :],
                                          wpool[:, WG0 + k * IFF + ic * 128:
                                                WG0 + k * IFF + (ic + 1) * 128],
                                          cq[:, k, tsl],
                                          start=(k == 0), stop=(k == KC - 1))
                        sGU.inc(mm)
                        for k in range(KC):
                            mm = t.matmul(pu[:, :],
                                          wpool[:, WU0 + k * IFF + ic * 128:
                                                WU0 + k * IFF + (ic + 1) * 128],
                                          cq[:, k, tsl],
                                          start=(k == 0), stop=(k == KC - 1))
                        sGU.inc(mm)
                    gu_mark = sGU.n

                    @block.scalar
                    def _(s, ic=ic, pb=pb, gum=gu_mark, gidx=gidx):
                        s.wait_ge(sGU.h, gum - 1)
                        s.wait_ge(sv.h, sxb_mark)
                        if gidx >= 2:
                            s.wait_ge(sv.h, humarks[gidx - 2])
                        sa.wait(s)
                        sa.inc(s.activation(gt[pb][:], ctx.psum(gidx % 2)[:, :],
                                            AF.Silu, scale=svg[:, ic:ic + 1]))
                    g_mark = sa.n

                    @block.vector
                    def _(v, ic=ic, ti=ti, pb=pb, gum=gu_mark, gm=g_mark, gidx=gidx):
                        v.wait_ge(sGU.h, gum)
                        v.wait_ge(sa.h, gm)
                        if gidx >= 2:
                            v.wait_ge(sHS.h, 16 * (gidx - 1))
                        sv.wait(v)
                        sv.inc(v.tensor_scalar(ut[pb][:], ctx.psum(2 + gidx % 2)[:, :],
                                               svu[:, ic:ic + 1], None, op0=ALU.mult))
                        sv.wait(v)
                        sv.inc(v.tensor_tensor(ht[pb][:], gt[pb][:], ut[pb][:],
                                               op=ALU.mult))
                        sv.wait(v)
                        sv.inc(v.tensor_reduce(
                            amx2[:, ic * NTILE + ti:ic * NTILE + ti + 1],
                            ht[pb][:], axis=AX.X, op=ALU.max,
                            apply_absolute_value=True))
                    gumarks.append(sv.n)

                    @block.sync
                    def _(sy, ic=ic, ti=ti, pb=pb, gidx=gidx, tsl=tsl):
                        sy.wait_ge(sv.h, gumarks[gidx])
                        sHS.inc(sy.dma_start(
                            out=hq_dram[ic * 128:(ic + 1) * 128, tsl],
                            in_=ht[pb][:]), 16)
                    humarks.append(gumarks[gidx])
            hu_done = sv.n
            hs_done = sHS.n

        amr = ctx.buf("p7_amr", [128, 1], F32)
        sCR2 = ctx.sem("p7_CR2")

        @block.vector
        def _(v):
            sv.wait(v)
            sv.inc(v.tensor_reduce(amr[:], amx2[:], axis=AX.X, op=ALU.max))
        am_mark = sv.n

        @block.gpsimd
        def _(g):
            g.wait_ge(sv.h, am_mark)
            sg.inc(g.tensor_reduce(am1[:], amr[:], axis=AX.C, op=ALU.max))
            g.wait_ge(sg.h, sg.n)
            sCC.inc(g.dma_start(out=cc2_in[:], in_=am1[:]), 16)
            g.wait_ge(sCC.h, sCC.n)
            g.collective_compute(
                "AllReduce", ALU.max,
                replica_groups=[list(range(NCORES))],
                ins=[cc2_in.opt()], outs=[cc2_out.opt()],
            ).then_inc(sCR2.h, 1)
            sCR2.n = max(sCR2.n, 1)
            g.wait_ge(sCR2.h, 1)
            sCCL.inc(g.dma_start(out=gmax_sb[:], in_=cc2_out[:]), 16)
        hg_mark = sCCL.n

        @block.scalar
        def _(s):
            s.wait_ge(sCCL.h, hg_mark)
            sa.wait(s)
            sa.inc(s.activation(srow[:, 0:1], clip_sb[:, 1:2], AF.Sigmoid))
        sig2_mark = sa.n

        @block.vector
        def _(v):
            v.wait_ge(sa.h, sig2_mark)
            sv.wait(v)
            sv.inc(v.tensor_tensor(srow[:, 0:1], srow[:, 0:1], gmax_sb[:],
                                   op=ALU.mult))
            sv.wait(v)
            sv.inc(v.tensor_scalar(srow[:, 0:1], srow[:, 0:1], 1.0 / 127.0, None,
                                   op0=ALU.mult))
            sv.wait(v)
            sv.inc(v.reciprocal(srow[:, 1:2], srow[:, 0:1]))
        sh_mark = sv.n

        @block.tensor
        def _(t):
            t.wait_ge(sv.h, sh_mark)
            cst['_sem'].wait(t)
            mm = t.matmul(ctx.psum(6)[:, 0:2], cst['onesr_f'][:], srow[:, 0:2],
                          start=True, stop=True)
            ctx.sem("p7_BH").inc(mm)

        @block.vector
        def _(v):
            v.wait_ge(ctx.sem("p7_BH").h, ctx.sem("p7_BH").n)
            sv.wait(v)
            sv.inc(v.tensor_copy(shb[:], ctx.psum(6)[:, 0:2]))
            sv.wait(v)
            sv.inc(v.tensor_scalar(svd[:], qsc[:, 2 * ICH:], shb[:, 0:1], None,
                                   op0=ALU.mult))
        shb_mark = sv.n

        @block.sync
        def _(sy):
            sy.wait_ge(sGU.h, sGU.n)
            for icc in range(ICH):
                sWT.inc(sy.dma_start_transpose(
                    out=wpool[:, icc * D:(icc + 1) * D],
                    in_=wdq[:, icc * 128:(icc + 1) * 128]), 16)
        wd_mark = sWT.n

        dmarks = []
        for ti in range(NTILE):
            tsl = slice(ti * 512, (ti + 1) * 512)
            hl_marks = []
            for icc in range(ICH):
                pb = icc % 2

                @block.sync
                def _(sy, ti=ti, icc=icc, pb=pb, tsl=tsl):
                    if ti == 0 and icc == 0:
                        sy.wait_ge(sHS.h, hs_done)
                    gl = len(hqmarks)
                    if gl >= 2:
                        sy.wait_ge(sv.h, hqmarks[gl - 2])
                    sHL.inc(sy.dma_start(
                        out=ht[pb][:],
                        in_=hq_dram[icc * 128:(icc + 1) * 128, tsl]), 16)
                hl_mark = sHL.n

                @block.vector
                def _(v, icc=icc, pb=pb, hlm=hl_mark):
                    v.wait_ge(sHL.h, hlm)
                    v.wait_ge(sv.h, shb_mark)
                    sv.wait(v)
                    sv.inc(v.tensor_scalar(ht[pb][:], ht[pb][:], shb[:, 1:2],
                                           None, op0=ALU.mult))
                    sv.wait(v)
                    sv.inc(v.tensor_scalar(ht[pb][:], ht[pb][:], RND_C, RND_C,
                                           op0=ALU.add, op1=ALU.subtract))
                    sv.wait(v)
                    sv.inc(v.tensor_scalar(hqt[pb][:], ht[pb][:], -128.0, 127.0,
                                           op0=ALU.max, op1=ALU.min))
                hqmarks.append(sv.n)
                hq_mark = sv.n

                @block.tensor
                def _(t, ti=ti, icc=icc, pb=pb, hqm=hq_mark):
                    t.wait_ge(sv.h, hqm)
                    if ti == 0 and icc == 0:
                        t.wait_ge(sWT.h, wd_mark)
                    if icc == 0 and ti >= 1:
                        t.wait_ge(sa.h, dmarks[ti - 1])
                    mm = None
                    for dcc in range(KC):
                        mm = t.matmul(ctx.psum(dcc)[:, :],
                                      wpool[:, icc * D + dcc * 128:
                                            icc * D + (dcc + 1) * 128],
                                      hqt[pb][:],
                                      start=(icc == 0), stop=(icc == ICH - 1))
                    if icc == ICH - 1:
                        sDM.inc(mm)

            dm_mark = sDM.n

            for dcc in range(KC):
                gy = ti * KC + dcc
                pb2 = gy % 2

                @block.sync
                def _(sy, ti=ti, dcc=dcc, pb2=pb2, gy=gy, tsl=tsl):
                    if gy >= 2:
                        sy.wait_ge(sv.h, outmarks[gy - 2])
                        sy.wait_ge(sOS.h, 16 * (gy - 1))
                    sRL.inc(sy.dma_start(out=x1t[pb2][:],
                                         in_=fmview(x1)[:, dcc, tsl]), 16)
                    sRL.inc(sy.dma_start(out=s1t[pb2][:],
                                         in_=fmview(s1)[:, dcc, tsl]), 16)
                rl_mark = sRL.n

                @block.scalar
                def _(s, ti=ti, dcc=dcc, pb2=pb2, dmm=dm_mark, gy=gy):
                    s.wait_ge(sDM.h, dmm)
                    s.wait_ge(sv.h, shb_mark)
                    if gy >= 2:
                        s.wait_ge(sv.h, outmarks[gy - 2])
                    sa.wait(s)
                    sa.inc(s.activation(gt[pb2][:], ctx.psum(dcc)[:, :], AF.Silu,
                                        scale=svd[:, dcc:dcc + 1]))
                ev_mark = sa.n

                @block.vector
                def _(v, pb2=pb2, evm=ev_mark, rlm=rl_mark, gy=gy):
                    v.wait_ge(sa.h, evm)
                    v.wait_ge(sRL.h, rlm)
                    sv.wait(v)
                    sv.inc(v.tensor_tensor(ott[pb2][:], gt[pb2][:], s1t[pb2][:],
                                           op=ALU.mult))
                    sv.wait(v)
                    sv.inc(v.tensor_tensor(ott[pb2][:], ott[pb2][:], x1t[pb2][:],
                                           op=ALU.add))
                outmarks.append(sv.n)

                @block.sync
                def _(sy, dcc=dcc, pb2=pb2, gy=gy, tsl=tsl):
                    sy.wait_ge(sv.h, outmarks[gy])
                    sOS.inc(sy.dma_start(out=fmview(out_fm)[:, dcc, tsl],
                                         in_=ott[pb2][:]), 16)
            dmarks.append(sa.n)

        @block.sync
        def _(sy):
            sy.wait_ge(sOS.h, sOS.n)
    return ctx


_CTX_CACHE = {}


def _get_ctx(name, builder, *args):
    if name not in _CTX_CACHE:
        _CTX_CACHE[name] = builder(*args)
    return _CTX_CACHE[name]


def _run(ctx, state):
    in_maps = []
    for c in range(NCORES):
        m = {}
        for nm in ctx.reads:
            m[nm] = state[c][nm]
        in_maps.append(m)
    res = run_bass_kernel_spmd(ctx.nc, in_maps, core_ids=list(range(NCORES)))
    for c in range(NCORES):
        for nm in ctx.writes:
            state[c][nm] = res.results[c][nm]


def kernel(L0_attn_x, L0_attn_log_lambda, L0_attn_b_w, L0_attn_b_b, L0_attn_c_w,
           L0_attn_c_b, L0_ffn_ln1_w, L0_ffn_ln1_b, L0_ffn_ln2_w, L0_ffn_ln2_b,
           L0_ffn_wq, L0_ffn_wk, L0_ffn_wv, L0_ffn_wo, L0_ffn_bq, L0_ffn_bk,
           L0_ffn_bv, L0_ffn_bo, L0_ffn_gate_w, L0_ffn_gate_b, L0_ffn_e_w1,
           L0_ffn_e_b1, L0_ffn_e_w2, L0_ffn_e_b2, L1_attn_W1, L1_attn_W2,
           L1_ffn_clip_x, L1_ffn_clip_down, L1_ffn_w_gate, L1_ffn_w_up,
           L1_ffn_w_down):
    f = np.float32

    def C(a):
        return np.ascontiguousarray(a, dtype=f)

    vecs = np.zeros((128, NV), f)

    def setv(name, arr, idx=0, n=None):
        arr = np.asarray(arr, f).reshape(-1)
        nn = arr.shape[0] // 128
        vecs[:, VO[name] + idx:VO[name] + idx + nn] = arr.reshape(nn, 128).T

    setv('loglam', L0_attn_log_lambda)
    setv('b_b', L0_attn_b_b)
    setv('c_b', L0_attn_c_b)
    for l in range(NL):
        setv('ln1_w', L0_ffn_ln1_w[l], l * KC)
        setv('ln1_b', L0_ffn_ln1_b[l], l * KC)
        setv('ln2_w', L0_ffn_ln2_w[l], l * KC)
        setv('ln2_b', L0_ffn_ln2_b[l], l * KC)
        setv('bq', L0_ffn_bq[l], l * KC)
        setv('bk', L0_ffn_bk[l], l * KC)
        setv('bv', L0_ffn_bv[l], l * KC)
        setv('bo', L0_ffn_bo[l], l * KC)
        for e in range(NEXP):
            setv('e_b1', L0_ffn_e_b1[l, e], (l * NEXP + e) * MC)
            setv('e_b2', L0_ffn_e_b2[l, e], (l * NEXP + e) * KC)

    shared = {
        'vecs': vecs,
        'bwT': C(np.asarray(L0_attn_b_w, f).T),
        'cwT': C(np.asarray(L0_attn_c_w, f).T),
        'attw': C(np.stack([np.stack([L0_ffn_wq[l], L0_ffn_wk[l],
                                      L0_ffn_wv[l], L0_ffn_wo[l]])
                            for l in range(NL)])),
        'e_w1': C(L0_ffn_e_w1),
        'e_w2': C(L0_ffn_e_w2),
        'W1T': C(np.asarray(L1_attn_W1, f).T),
        'W2T': C(np.asarray(L1_attn_W2, f).T),
        'wg': C(L1_ffn_w_gate),
        'wu': C(L1_ffn_w_up),
        'wd': C(L1_ffn_w_down),
        'gatew': C(L0_ffn_gate_w),
        'gate_b': C(np.asarray(L0_ffn_gate_b, f).T),
        'b2e': C(L0_ffn_e_b2),
        'clip': C(np.stack([np.asarray(L1_ffn_clip_x, f).reshape(()),
                            np.asarray(L1_ffn_clip_down, f).reshape(())])[None]),
    }

    x = np.asarray(L0_attn_x, f)
    state = []
    for c in range(NCORES):
        d = dict(shared)
        xc = x[2 * c:2 * c + 2]
        d['x_fm'] = C(xc.transpose(2, 0, 1).reshape(D, T))
        state.append(d)

    ctx = _get_ctx('full', build_full)
    _run(ctx, state)

    out = np.empty((16, SEQ, D), f)
    for c in range(NCORES):
        o = state[c]['out_fm']
        out[2 * c:2 * c + 2] = o.reshape(D, 2, SEQ).transpose(1, 2, 0)
    return out


def emit_ln1(ctx, l, yin):
    y_in = ctx.internal(yin, [D, T], F32)
    zb = ctx.internal("zb", [D, T], BF16)
    emit_ln(ctx, "ln1", src=y_in, out_bf16=zb, vs=ctx.vecs,
            wname='ln1_w', bname='ln1_b', lidx=l)

    @ctx.block.sync
    def _(sy):
        sy.wait_ge(ctx.sem("ln_Out").h, ctx.sem("ln_Out").n)


def build_full():
    ctx = Ctx("full")
    nc = ctx.nc
    with nc.Block() as block:
        ctx.block = block
        emit_consts(ctx)
        load_vecs(ctx)
        ctx.psum(0)
        ctx.arena_start()

        ychain = ['xn0f', 'ya0', 'ym0', 'ya1', 'yffn1']
        ctx.join_exclude = {'p0_Cast'}
        ctx.begin_phase('p0')
        emit_p0(ctx)
        ctx.begin_phase('p12', stack=True)
        emit_p12(ctx)
        for l in range(NL):
            ctx.begin_phase(f'l{l}a')
            emit_ln1(ctx, l, ychain[2 * l])
            ctx.begin_phase(f'l{l}b')
            emit_attn(ctx, l, ychain[2 * l], ychain[2 * l + 1])
            ctx.begin_phase(f'l{l}c')
            emit_ln2gate(ctx, l, ychain[2 * l + 1])
            ctx.begin_phase(f'l{l}d')
            emit_moe(ctx, l, ychain[2 * l + 1], ychain[2 * l + 2])
        ctx.begin_phase('p45')
        emit_p45(ctx)
        ctx.begin_phase('z1')
        emit_z1(ctx)
        ctx.begin_phase('p0q', stack=True)
        emit_p0_quant(ctx)
        ctx.begin_phase('p6')
        emit_p6(ctx)
        ctx.begin_phase('p7')
        emit_p7(ctx)
    return ctx
